# version 28
# baseline (speedup 1.0000x reference)
import numpy as np
from contextlib import ExitStack

import concourse.bass as bass
import concourse.mybir as mybir
from concourse.bass import IndirectOffsetOnAxis
from concourse.bass_utils import run_bass_kernel_spmd

F32 = mybir.dt.float32
BF16 = mybir.dt.bfloat16
I32 = mybir.dt.int32
U32 = mybir.dt.uint32
AF = mybir.ActivationFunctionType
ALU = mybir.AluOpType
AX = mybir.AxisListType

B, T, C = 4, 4096, 1024
E, CAP = 8, 2048
NCORE = 8
TPC = 2048
NTC = TPC // 128
NIC = C // 128
FF = 4096
NFC = FF // 128
TBL = 16896

DEBUG_STAGE = 0


class Buf:
    __slots__ = ("name", "w", "readers")

    def __init__(self, name):
        self.name = name
        self.w = None
        self.readers = []


class Prog:
    ENGINES = ("sync", "tensor", "vector", "scalar", "gpsimd")

    def __init__(self):
        self.q = {e: [] for e in self.ENGINES}
        self.cnt = {}

    def _tok(self, inc):
        sem, amt = inc
        self.cnt[sem] = self.cnt.get(sem, 0) + amt
        return (sem, self.cnt[sem])

    def emit(self, eng, builder, reads=(), writes=(), extra=(), inc=None,
             bookkeep=True):
        waits = set()
        for b in reads:
            if b.w is not None:
                waits.add(b.w)
        for b in writes:
            if b.w is not None:
                waits.add(b.w)
            for r in b.readers:
                waits.add(r)
        for t in extra:
            if t is not None:
                waits.add(t)
        tok = self._tok(inc) if inc is not None else None
        self.q[eng].append((builder, tuple(sorted(waits)), inc))
        if tok is not None and bookkeep:
            for b in reads:
                b.readers.append(tok)
            for b in writes:
                b.w = tok
                b.readers = []
        return tok

    def barrier(self, eng, toks):
        toks = tuple(t for t in toks if t is not None)
        if toks:
            self.q[eng].append((None, toks, None))

    def all_tokens(self, exclude=("cc",)):
        return [(s, v) for s, v in sorted(self.cnt.items()) if s not in exclude]

    def full_barrier(self, exclude=("cc",)):
        toks = self.all_tokens(exclude)
        for e in self.ENGINES:
            self.barrier(e, toks)

    def replay(self, eng_name, eng_obj, sems):
        for builder, waits, inc in self.q[eng_name]:
            for semname, val in waits:
                eng_obj.wait_ge(sems[semname], val)
            if builder is None:
                continue
            ins = builder(eng_obj)
            if inc is not None:
                sem, amt = inc
                ins.then_inc(sems[sem], amt)


def build_nc(debug_stage=DEBUG_STAGE):
    nc = bass.Bass("TRN2", target_bir_lowering=False, debug=False)
    P = Prog()

    def inp(name, shape, dt=F32):
        return nc.dram_tensor(name, shape, dt, kind="ExternalInput").ap()

    x_tm_h = inp("x_tm", [TPC, C])
    x_fm_h = inp("x_fm", [C, TPC])
    wqT_h = inp("wqT", [C, C]); wkT_h = inp("wkT", [C, C])
    wvT_h = inp("wvT", [C, C]); woT_h = inp("woT", [C, C])
    wo_h = inp("wo", [C, C])
    bq_cols_h = inp("bq_cols", [128, NIC])
    bo_cols_h = inp("bo_cols", [128, NIC])
    bk_row_h = inp("bk_row", [1, C]); bv_row_h = inp("bv_row", [1, C])
    bo_row_h = inp("bo_row", [1, C])
    wr_h = inp("wr", [C, E])
    w1e_h = inp("w1e", [C, FF]); w2e_h = inp("w2e", [FF, C])
    gid_h = inp("gid", [TPC, 1], I32)
    xfrow_h = inp("xfrow", [TPC, 1], I32)
    kvidx_h = inp("kvidx", [256, 1], I32)
    ltri_h = inp("ltri", [128, 128])
    onesb_row_h = inp("onesb_row", [1, 128], BF16)
    onesb_col_h = inp("onesb_col", [128, 1], BF16)
    ones_col_h = inp("ones_col", [128, 1])
    ones_1x128_h = inp("ones_1x128", [1, 128])
    iota8_h = inp("iota8", [128, E])
    trash_h = inp("trash", [128, 1])
    bdmask_h = inp("bdmask", [128, 128])
    hmask2_h = inp("hmask2", [128, 2])
    bmask_h = inp("bmask", [2, 128], BF16)
    ltme_h = inp("ltme", [NCORE, 1])
    iota132_h = inp("iota132", [128, 132])
    iota128f_h = inp("iota128f", [128, 128])
    gidf_h = inp("gidf", [TPC, 1])
    tabidx_h = inp("tabidx", [TPC, 1], I32)

    out_h = nc.dram_tensor("y", [TPC, C], F32, kind="ExternalOutput").ap()
    dbg = {}
    if debug_stage == 1:
        dbg["xln"] = nc.dram_tensor("dbg_xln", [TPC, C], F32, kind="ExternalOutput").ap()
    if debug_stage == 2:
        dbg["meta"] = nc.dram_tensor("dbg_meta", [TPC, 4], F32, kind="ExternalOutput").ap()
        dbg["table"] = nc.dram_tensor("dbg_table", [128, TBL // 128], I32, kind="ExternalOutput").ap()
        dbg["cnt"] = nc.dram_tensor("dbg_cnt", [NCORE, E], F32, kind="ExternalOutput").ap()
        dbg["lgs"] = nc.dram_tensor("dbg_lgs", [TPC, E], F32, kind="ExternalOutput").ap()
        dbg["rp"] = nc.dram_tensor("dbg_rp", [TPC, E], F32, kind="ExternalOutput").ap()
        dbg["mr"] = nc.dram_tensor("dbg_mr", [TPC, 4], F32, kind="ExternalOutput").ap()
        dbg["wowr"] = nc.dram_tensor("dbg_wowr", [128, E * NIC], BF16, kind="ExternalOutput").ap()
    if debug_stage == 3:
        dbg["ye"] = nc.dram_tensor("dbg_ye", [TPC, C], F32, kind="ExternalOutput").ap()

    cc_kv_in = nc.dram_tensor("cc_kv_in", [128, 1032], F32).ap()
    cc_kv_out = nc.dram_tensor("cc_kv_out", [128 * NCORE, 1032], F32).ap()
    cc_xf_in = nc.dram_tensor("cc_xf_in", [TPC, C], BF16).ap()
    cc_xf_out = nc.dram_tensor("cc_xf_out", [TPC * NCORE, C], BF16).ap()
    cc_cnt_in = nc.dram_tensor("cc_cnt_in", [1, E], F32).ap()
    cc_cnt_out = nc.dram_tensor("cc_cnt_out", [NCORE, E], F32).ap()
    table_d = nc.dram_tensor("table_d", [TBL, 1], I32).ap()
    table_2d = table_d.rearrange("(a b) 1 -> a b", a=128)
    xe_rows_d = nc.dram_tensor("xe_rows_d", [TPC, C], BF16).ap()
    cc_ye_in = nc.dram_tensor("cc_ye_in", [TPC, C], BF16).ap()
    cc_ye_out = nc.dram_tensor("cc_ye_out", [TPC * NCORE, C], BF16).ap()

    d_kvin = Buf("d_kvin"); d_kvout = Buf("d_kvout")
    d_xfin = Buf("d_xfin"); d_xfout = Buf("d_xfout")
    d_cntin = Buf("d_cntin"); d_cntout = Buf("d_cntout")
    d_tab = Buf("d_tab")
    d_xer = Buf("d_xer"); d_yein = Buf("d_yein"); d_yeout = Buf("d_yeout")
    d_out = Buf("d_out"); d_dbg = Buf("d_dbg")

    def dma(eng, out, in_, reads=(), writes=(), extra=(), sem=None):
        if sem is None:
            sem = "dS" if eng == "sync" else "dG"
        return P.emit(eng, lambda Eg, o=out, i=in_: Eg.dma_start(out=o, in_=i),
                      reads=reads, writes=writes, extra=extra, inc=(sem, 16))

    def dma_t(out, in_, reads=(), writes=(), extra=()):
        return P.emit("scalar",
                      lambda Eg, o=out, i=in_: Eg.dma_start(out=o, in_=i, transpose=True),
                      reads=reads, writes=writes, extra=extra, inc=("dT", 16))

    def idma(out, out_off, in_, in_off, reads=(), writes=(), extra=(), sem="dG"):
        return P.emit("gpsimd",
                      lambda Eg, o=out, oo=out_off, i=in_, io=in_off:
                      Eg.indirect_dma_start(o, oo, i, io),
                      reads=reads, writes=writes, extra=extra, inc=(sem, 16))

    def coll(kind, op, ins_ap, outs_ap, reads=(), writes=(), extra=()):
        return P.emit("gpsimd",
                      lambda Eg, k=kind, o=op, ia=ins_ap, oa=outs_ap:
                      Eg.collective_compute(k, o, replica_groups=[list(range(NCORE))],
                                            ins=[ia], outs=[oa]),
                      reads=reads, writes=writes, extra=extra, inc=("cc", 1))

    def mmgrp(out_ap, parts, reads=(), writes=(), extra=()):
        n = len(parts)
        tok = None
        for i, (l, r) in enumerate(parts):
            tok = P.emit("tensor",
                         lambda Eg, o=out_ap, l=l, r=r, s=(i == 0), e=(i == n - 1):
                         Eg.matmul(o, l, r, start=s, stop=e),
                         reads=reads if i == 0 else (),
                         writes=writes if i == 0 else (),
                         extra=extra if i == 0 else (),
                         inc=("tP", 1) if i == n - 1 else None,
                         bookkeep=False)
        for b in reads:
            b.readers.append(tok)
        for b in writes:
            b.w = tok
            b.readers = []
        return tok

    def vop(fn, reads=(), writes=(), extra=()):
        return P.emit("vector", fn, reads=reads, writes=writes, extra=extra, inc=("tV", 1))

    def sop(fn, reads=(), writes=(), extra=()):
        return P.emit("scalar", fn, reads=reads, writes=writes, extra=extra, inc=("tA", 1))

    es = ExitStack()
    def sb(name, shape, dt=F32):
        return es.enter_context(nc.sbuf_tensor("s_" + name, shape, dt))

    lgs = [sb(f"lgs{i}", [128, E]) for i in range(NTC)]; b_lgs = [Buf(f"lgs{i}") for i in range(NTC)]
    oh_all = sb("oh_all", [128, NTC * E])
    onehot = [oh_all[:, i * E:(i + 1) * E] for i in range(NTC)]
    b_oh = [Buf(f"oh{i}") for i in range(NTC)]
    rank_all = sb("rank_all", [128, NTC * E])
    rank_sb = [rank_all[:, i * E:(i + 1) * E] for i in range(NTC)]
    b_rk = [Buf(f"rk{i}") for i in range(NTC)]
    ei_all = sb("ei_all", [128, NTC])
    eidx_f = [ei_all[:, i:i + 1] for i in range(NTC)]
    b_ei = [Buf(f"ei{i}") for i in range(NTC)]
    ptk_all = sb("ptk_all", [128, NTC])
    ptok = [ptk_all[:, i:i + 1] for i in range(NTC)]
    b_ptk = [Buf(f"ptk{i}") for i in range(NTC)]
    psc_all = sb("psc_all", [128, NTC]); b_psca = Buf("psc_all")
    pscale = [psc_all[:, i:i + 1] for i in range(NTC)]
    b_psc = [b_psca] * NTC
    gsl_all = sb("gsl_all", [128, NTC]); b_gsla = Buf("gsl_all")
    gslot = [gsl_all[:, i:i + 1] for i in range(NTC)]
    b_gsl = [b_gsla] * NTC
    kee_all = sb("kee_all", [128, NTC]); b_keea = Buf("kee_all")
    keepc = [kee_all[:, i:i + 1] for i in range(NTC)]
    b_kee = [b_keea] * NTC
    sx_all = sb("sx_all", [128, NTC], I32); b_sxa = Buf("sx_all")
    yrw_f = sb("yrw_f", [128, NTC]); b_yrwf = Buf("yrw_f")
    hi_i_all = sb("hi_i_all", [128, NTC], I32); b_hiia = Buf("hi_i_all")
    lo_i_all = sb("lo_i_all", [128, NTC], I32); b_loia = Buf("lo_i_all")
    hi_f_all = sb("hi_f_all", [128, NTC]); b_hifa = Buf("hi_f_all")
    lo_f_all = sb("lo_f_all", [128, NTC]); b_lofa = Buf("lo_f_all")
    cm8a = sb("cm8a", [128, NTC * E]); b_cm8a = Buf("cm8a")
    cm8b = sb("cm8b", [128, NTC * E]); b_cm8b = Buf("cm8b")
    cm1a = sb("cm1a", [128, NTC]); b_cm1a = Buf("cm1a")
    cm1b = sb("cm1b", [128, NTC]); b_cm1b = Buf("cm1b")
    cm1c = sb("cm1c", [128, NTC]); b_cm1c = Buf("cm1c")
    yrow = [sb(f"yrw{i}", [128, 1], I32) for i in range(NTC)]; b_yrw = [Buf(f"yrw{i}") for i in range(NTC)]
    gid_sb = [sb(f"gid{i}", [128, 1], I32) for i in range(NTC)]; b_gid = [Buf(f"gid{i}") for i in range(NTC)]
    xfr_sb = [sb(f"xfr{i}", [128, 1], I32) for i in range(NTC)]
    b_xfr = [Buf(f"xfr{i}") for i in range(NTC)]
    crow = [sb(f"crw{i}", [1, E]) for i in range(2)]; b_crw = [Buf(f"crw{i}") for i in range(2)]
    ccol = sb("ccol", [1, E]); b_ccl = Buf("ccol")
    cnt_row = sb("cnt_row", [1, E]); b_cnr = Buf("cnt_row")
    cnt8 = sb("cnt8", [NCORE, E]); b_cn8 = Buf("cnt8")
    base_bc = sb("base_bc", [128, E]); b_bbc = Buf("base_bc")
    base_row = sb("base_row", [1, E]); b_bsr = Buf("base_row")
    wrsum_row = sb("wrsum_row", [1, E]); b_wsr = Buf("wrsum_row")
    wrsum_bc = sb("wrsum_bc", [128, E]); b_wsb = Buf("wrsum_bc")
    bowr_row = sb("bowr_row", [1, E]); b_bwr = Buf("bowr_row")
    wowrb = [sb(f"wowrb{j}", [128, E], BF16) for j in range(NIC)]
    b_wwb = [Buf(f"wowrb{j}") for j in range(NIC)]
    RP = [sb(f"RP{i}", [128, E]) for i in range(NTC)]; b_RP = [Buf(f"RP{i}") for i in range(NTC)]
    top8 = sb("top8", [128, E]); b_t8 = Buf("top8")
    idx8 = sb("idx8", [128, E], U32); b_i8 = Buf("idx8")
    tmp8a = sb("tmp8a", [128, E]); b_m8a = Buf("tmp8a")
    tmp8b = sb("tmp8b", [128, E]); b_m8b = Buf("tmp8b")
    tmp1a = sb("tmp1a", [128, 1]); b_m1a = Buf("tmp1a")
    tmp1b = sb("tmp1b", [128, 1]); b_m1b = Buf("tmp1b")
    tmp1c = sb("tmp1c", [128, 1]); b_m1c = Buf("tmp1c")
    ln_c = sb("ln_c", [128, 4]); b_lnc = Buf("ln_c")
    tbix = [sb(f"tbix{i}", [128, 1], I32) for i in range(NTC)]
    b_tbix = [Buf(f"tbix{i}") for i in range(NTC)]
    tabsb = [sb(f"tab{i}", [128, 1], I32) for i in range(NTC)]
    b_tab = [Buf(f"tab{i}") for i in range(NTC)]
    bqc = sb("bqc", [128, NIC]); b_bqc = Buf("bqc")
    boc = sb("boc", [128, NIC]); b_boc = Buf("boc")
    bkb = sb("bkb", [1, C], BF16); b_bkb = Buf("bkb")
    bvb = sb("bvb", [1, C], BF16); b_bvb = Buf("bvb")
    bob = sb("bob", [1, C], BF16); b_bob = Buf("bob")
    onesb = sb("onesb", [1, 128], BF16); b_onesb = Buf("onesb")
    onesb_col = sb("onesb_col", [128, 1], BF16); b_onesbc = Buf("onesb_col")
    ones_col = sb("ones_col", [128, 1]); b_onesc = Buf("ones_col")
    ones_1x128 = sb("ones_1x128", [1, 128]); b_o1 = Buf("ones_1x128")
    hmask2 = sb("hmask2", [128, 2]); b_hm2 = Buf("hmask2")
    bmask = sb("bmask", [2, 128], BF16); b_bm = Buf("bmask")
    ltri = sb("ltri", [128, 128]); b_ltr = Buf("ltri")
    iota8 = sb("iota8", [128, E]); b_io8 = Buf("iota8")
    trash = sb("trash", [128, 1]); b_trh = Buf("trash")
    ltme = sb("ltme", [NCORE, 1]); b_ltm = Buf("ltme")
    wrf = [sb(f"wrf{j}", [128, E]) for j in range(NIC)]
    b_wrf = [Buf(f"wrf{j}") for j in range(NIC)]
    wrbs = sb("wrbs", [128, E * NIC], BF16); b_wrb = Buf("wrbs")
    bowr_bf = sb("bowr_bf", [1, E], BF16); b_bwrb = Buf("bowr_bf")
    kvidx = [sb(f"kvix{i}", [128, 1], I32) for i in range(2)]
    b_kvx = [Buf(f"kvix{i}") for i in range(2)]

    esA = ExitStack()
    def sbA(name, shape, dt=F32):
        return esA.enter_context(nc.sbuf_tensor("sa_" + name, shape, dt))

    Xs = [sbA(f"X{i}", [128, C]) for i in range(2)]; bXs = [Buf(f"X{i}") for i in range(2)]
    XF = [sbA(f"XF{j}", [128, TPC], BF16) for j in range(NIC)]
    bXF = [Buf(f"XF{j}") for j in range(NIC)]
    QT = [sbA(f"QT{j}", [128, C], BF16) for j in range(NIC)]; bQT = [Buf(f"QT{j}") for j in range(NIC)]
    KT = [sbA(f"KT{j}", [128, C], BF16) for j in range(NIC)]; bKT = [Buf(f"KT{j}") for j in range(NIC)]
    VT = [sbA(f"VT{j}", [128, C], BF16) for j in range(NIC)]; bVT = [Buf(f"VT{j}") for j in range(NIC)]
    OT = [sbA(f"OT{j}", [128, C], BF16) for j in range(NIC)]; bOT = [Buf(f"OT{j}") for j in range(NIC)]
    QB = [sbA(f"QB{p}", [128, TPC], BF16) for p in range(8)]; bQB = [Buf(f"QB{p}") for p in range(8)]
    kbt = [sbA(f"kbt{i}", [128, C], BF16) for i in range(2)]; b_kb = [Buf(f"kbt{i}") for i in range(2)]
    vbt = [sbA(f"vbt{i}", [128, C], BF16) for i in range(2)]; b_vb = [Buf(f"vbt{i}") for i in range(2)]
    sc = [sbA(f"sc{i}", [128, 512]) for i in range(3)]; bsc = [Buf(f"sc{i}") for i in range(3)]
    rt_2 = [sbA("rt", [128, 1040]), None]
    b_rt_2 = [Buf("rt0"), Buf("rt1")]
    rt = rt_2[0]; b_rt = b_rt_2[0]
    rt_2[1] = sbA("rt1", [128, C])
    xfb_2 = [sbA(f"xfb{i}", [128, C], BF16) for i in range(2)]
    b_xfb_2 = [Buf(f"xfb{i}") for i in range(2)]
    lnc_2 = [sbA(f"lnc2_{i}", [128, 8]) for i in range(2)]
    b_lnc_2 = [Buf(f"lnc2_{i}") for i in range(2)]
    t8_2 = [sbA(f"t8_{i}", [128, E]) for i in range(2)]
    b_t8_2 = [Buf(f"t8_{i}") for i in range(2)]
    m8a_2 = [sbA(f"m8a_{i}", [128, E]) for i in range(2)]
    b_m8a_2 = [Buf(f"m8a_{i}") for i in range(2)]
    m8b_2 = [sbA(f"m8b_{i}", [128, E]) for i in range(2)]
    b_m8b_2 = [Buf(f"m8b_{i}") for i in range(2)]
    m1b_2 = [sbA(f"m1b_{i}", [128, 1]) for i in range(2)]
    b_m1b_2 = [Buf(f"m1b_{i}") for i in range(2)]
    kv_sb = sbA("kv_sb", [128, 1032]); b_kvs = Buf("kv_sb")
    kvb = sbA("kvb", [128, C], BF16); b_kvb = Buf("kvb")
    bdmask = sbA("bdmask", [128, 128]); b_bdm = Buf("bdmask")
    ks2 = sbA("ks2", [128, 16], BF16); b_ks2 = Buf("ks2")
    rd_sb4 = sbA("rd_sb4", [34, TPC])
    rd_sb2 = [rd_sb4[0:2, :], rd_sb4[32:34, :]]
    b_rd2 = [Buf(f"rd_sb{i}") for i in range(2)]
    rdb_sb2 = [sbA(f"rdb_sb{i}", [2, TPC], BF16) for i in range(2)]
    b_rdb2 = [Buf(f"rdb_sb{i}") for i in range(2)]
    iota132 = sbA("iota132", [128, 132]); b_i132 = Buf("iota132")
    iota128f = sbA("iota128f", [128, 128]); b_i128 = Buf("iota128f")
    gidf_all = sbA("gidf_all", [128, NTC])
    gidf_sb = [gidf_all[:, i:i + 1] for i in range(NTC)]
    b_gidf = [Buf(f"gidf{i}") for i in range(NTC)]
    hi_i = sbA("hi_i", [128, 1], I32); b_hii = Buf("hi_i")
    lo_i = sbA("lo_i", [128, 1], I32); b_loi = Buf("lo_i")
    hi_f = sbA("hi_f", [128, 1]); b_hif = Buf("hi_f")
    lo_f = sbA("lo_f", [128, 1]); b_lof = Buf("lo_f")
    Afm = sbA("Afm", [128, 132]); b_Afm = Buf("Afm")
    Bfm = sbA("Bfm", [128, 128]); b_Bfm = Buf("Bfm")
    Bg = sbA("Bg", [128, 128]); b_Bg = Buf("Bg")
    tabsb_i = sbA("tabsb_i", [128, 132], I32); b_tbi = Buf("tabsb_i")
    tabsb_f = sbA("tabsb_f", [128, 132]); b_tbf = Buf("tabsb_f")

    esPA = ExitStack()
    kps2 = [esPA.enter_context(nc.psum_tensor(f"kps{i}", [128, C], F32)) for i in range(2)]
    b_kps2 = [Buf(f"kps{i}") for i in range(2)]
    vps = esPA.enter_context(nc.psum_tensor("vps", [128, 512], F32)); b_vps = Buf("vps")
    kvps = esPA.enter_context(nc.psum_tensor("kvps", [128, C], F32)); b_kvq = Buf("kvps")
    ksps = esPA.enter_context(nc.psum_tensor("ksps", [128, NIC], F32)); b_ksq = Buf("ksps")

    for i in range(NTC):
        dma("sync", gid_sb[i][:, :], gid_h[i * 128:(i + 1) * 128, :], writes=[b_gid[i]])
        dma("sync", xfr_sb[i][:, :], xfrow_h[i * 128:(i + 1) * 128, :], writes=[b_xfr[i]])
    for j in range(NIC):
        dma("gpsimd", XF[j][:, :], x_fm_h[j * 128:(j + 1) * 128, :], writes=[bXF[j]])
        dma("gpsimd", KT[j][:, :], wkT_h[j * 128:(j + 1) * 128, :], writes=[bKT[j]])
        dma("gpsimd", VT[j][:, :], wvT_h[j * 128:(j + 1) * 128, :], writes=[bVT[j]])
        dma("sync", wrf[j][:, :], wr_h[j * 128:(j + 1) * 128, :], writes=[b_wrf[j]])
        tw = dma("gpsimd", wrbs[:, j * E:(j + 1) * E], wr_h[j * 128:(j + 1) * 128, :],
                 writes=[b_wrb] if j == 0 else [],
                 extra=() if j == 0 else (b_wrb.w,), sem="dB")
        if j == NIC - 1:
            b_wrb.w = tw
    dma("sync", bqc[:, :], bq_cols_h[:, :], writes=[b_bqc])
    dma("sync", boc[:, :], bo_cols_h[:, :], writes=[b_boc])
    dma("gpsimd", bkb[:, :], bk_row_h[:, :], writes=[b_bkb], sem="dB")
    dma("gpsimd", bvb[:, :], bv_row_h[:, :], writes=[b_bvb], sem="dB")
    dma("gpsimd", bob[:, :], bo_row_h[:, :], writes=[b_bob], sem="dB")
    dma("sync", onesb[:, :], onesb_row_h[:, :], writes=[b_onesb])
    dma("sync", onesb_col[:, :], onesb_col_h[:, :], writes=[b_onesbc])
    dma("sync", ones_col[:, :], ones_col_h[:, :], writes=[b_onesc])
    dma("sync", ones_1x128[:, :], ones_1x128_h[:, :], writes=[b_o1])
    dma("sync", bdmask[:, :], bdmask_h[:, :], writes=[b_bdm])
    dma("sync", hmask2[:, :], hmask2_h[:, :], writes=[b_hm2])
    dma("sync", bmask[:, :], bmask_h[:, :], writes=[b_bm])
    dma("sync", ltri[:, :], ltri_h[:, :], writes=[b_ltr])
    dma("sync", iota8[:, :], iota8_h[:, :], writes=[b_io8])
    dma("sync", trash[:, :], trash_h[:, :], writes=[b_trh])
    dma("sync", ltme[:, :], ltme_h[:, :], writes=[b_ltm])
    for i in range(2):
        dma("sync", kvidx[i][:, :], kvidx_h[i * 128:(i + 1) * 128, :], writes=[b_kvx[i]])
    dma("sync", iota132[:, :], iota132_h[:, :], writes=[b_i132])
    dma("sync", iota128f[:, :], iota128f_h[:, :], writes=[b_i128])
    for i in range(NTC):
        dma("sync", gidf_sb[i], gidf_h[i * 128:(i + 1) * 128, :], writes=[b_gidf[i]])
        dma("sync", tbix[i][:, :], tabidx_h[i * 128:(i + 1) * 128, :], writes=[b_tbix[i]])

    def A_proj(tc):
        pi = tc % 2
        kps = kps2[pi]; b_kps = b_kps2[pi]
        xsl = [XF[ic][:, tc * 128:(tc + 1) * 128] for ic in range(NIC)]
        for jh in range(2):
            sl = slice(jh * 512, (jh + 1) * 512)
            mmgrp(kps[:, sl],
                  [(xsl[ic], KT[ic][:, sl]) for ic in range(NIC)] +
                  [(onesb[:, 0:128], bkb[:, sl])],
                  reads=bXF + bKT + [b_onesb, b_bkb], writes=[b_kps])
        for jh in range(2):
            sl = slice(jh * 512, (jh + 1) * 512)
            mmgrp(vps[:, :],
                  [(xsl[ic], VT[ic][:, sl]) for ic in range(NIC)] +
                  [(onesb[:, 0:128], bvb[:, sl])],
                  reads=bXF + bVT + [b_onesb, b_bvb], writes=[b_vps])
            tve = vop(lambda Eg, sl=sl, pi=pi: Eg.tensor_copy(vbt[pi][:, sl], vps[:, :]),
                      reads=[b_vps], writes=[b_vb[pi]] if jh == 0 else [],
                      extra=() if jh == 0 else (b_vb[pi].w,))
            if jh == 1:
                b_vb[pi].w = tve
        for jh in range(2):
            sl = slice(jh * 512, (jh + 1) * 512)
            t1 = vop(lambda Eg, sl=sl, kps=kps: Eg.tensor_scalar(
                sc[0][:, :], kps[:, sl], 0.0, None, ALU.min),
                reads=[b_kps], writes=[bsc[0]])
            t2 = sop(lambda Eg: Eg.activation(out=sc[1][:, :], in_=sc[0][:, :], func=AF.Exp),
                     reads=[bsc[0]], writes=[bsc[1]])
            t3 = sop(lambda Eg, sl=sl, kps=kps: Eg.activation(
                out=sc[2][:, :], in_=kps[:, sl], func=AF.Relu),
                reads=[b_kps], writes=[bsc[2]])
            tke = vop(lambda Eg, sl=sl, pi=pi: Eg.tensor_add(
                kbt[pi][:, sl], sc[2][:, :], sc[1][:, :]),
                reads=[bsc[1], bsc[2]], writes=[b_kb[pi]] if jh == 0 else [],
                extra=() if jh == 0 else (b_kb[pi].w,))
            if jh == 1:
                b_kb[pi].w = tke

    def A_acc(tc):
        pi = tc % 2
        for p in range(8):
            sl = slice(p * 128, (p + 1) * 128)
            tok = P.emit("tensor",
                         lambda Eg, o=kvps[:, sl], l=kbt[pi][:, sl], r=vbt[pi][:, sl],
                                s=(tc == 0 and p in (0, 4)), e=(tc == NTC - 1):
                         Eg.matmul(o, l, r, start=s, stop=e),
                         reads=[b_kb[pi], b_vb[pi]],
                         writes=[b_kvq] if (tc == 0 and p == 0) else [],
                         inc=("tP", 1), bookkeep=False)
            b_kb[pi].readers.append(tok); b_vb[pi].readers.append(tok)
            if tc == 0 and p == 0:
                b_kvq.w = tok
            if tc == NTC - 1 and p == 7:
                b_kvq.w = tok
        for jc in range(NIC):
            tok = P.emit("tensor",
                         lambda Eg, o=ksps[:, jc:jc + 1],
                                l=kbt[pi][:, jc * 128:(jc + 1) * 128],
                                r=onesb_col[:, :], s=(tc == 0 and jc == 0), e=(tc == NTC - 1):
                         Eg.matmul(o, l, r, start=s, stop=e),
                         reads=[b_kb[pi], b_onesbc],
                         writes=[b_ksq] if (tc == 0 and jc == 0) else [],
                         inc=("tP", 1), bookkeep=False)
            b_kb[pi].readers.append(tok)
            if (tc == 0 and jc == 0) or (tc == NTC - 1 and jc == NIC - 1):
                b_ksq.w = tok

    for tc in range(NTC):
        A_proj(tc)
        if tc > 0:
            A_acc(tc - 1)
    A_acc(NTC - 1)

    vop(lambda Eg: Eg.tensor_copy(kv_sb[:, 0:1024], kvps[:, :]), reads=[b_kvq], writes=[b_kvs])
    tks = vop(lambda Eg: Eg.tensor_copy(kv_sb[:, 1024:1032], ksps[:, :]),
              reads=[b_ksq], extra=(b_kvs.w,))
    b_kvs.w = tks
    dma("gpsimd", cc_kv_in[:, :], kv_sb[:, :], reads=[b_kvs], writes=[d_kvin], sem="dKV")
    coll("AllGather", ALU.bypass, cc_kv_in[:, :], cc_kv_out[:, :],
         reads=[d_kvin], writes=[d_kvout])

    for j in range(NIC):
        dma("gpsimd", QT[j][:, :], wqT_h[j * 128:(j + 1) * 128, :], writes=[bQT[j]])
        dma("gpsimd", OT[j][:, :], woT_h[j * 128:(j + 1) * 128, :], writes=[bOT[j]])

    P.barrier("tensor", P.all_tokens())
    esPA.close()
    esPB1 = ExitStack()
    qps2 = [esPB1.enter_context(nc.psum_tensor(f"qps{i}", [128, 512], F32)) for i in range(2)]
    b_qps2 = [Buf(f"qps{i}") for i in range(2)]
    rpps = esPB1.enter_context(nc.psum_tensor("rpps", [128, E], F32)); b_rpq = Buf("rpps")
    wwps = esPB1.enter_context(nc.psum_tensor("wwps", [128, E * NIC], F32)); b_wwq = Buf("wwps")

    mmgrp(rpps[0:1, :], [(ones_col[:, :], wrf[j][:, :]) for j in range(NIC)],
          reads=[b_onesc] + b_wrf, writes=[b_rpq])
    vop(lambda Eg: Eg.tensor_copy(wrsum_row[:, :], rpps[0:1, :]), reads=[b_rpq], writes=[b_wsr])
    mmgrp(rpps[0:1, :], [(boc[:, j:j + 1], wrf[j][:, :]) for j in range(NIC)],
          reads=[b_boc] + b_wrf, writes=[b_rpq])
    vop(lambda Eg: Eg.tensor_copy(bowr_row[:, :], rpps[0:1, :]), reads=[b_rpq], writes=[b_bwr])
    vop(lambda Eg: Eg.tensor_copy(bowr_bf[:, :], bowr_row[:, :]), reads=[b_bwr], writes=[b_bwrb])
    mmgrp(rpps[:, :], [(ones_1x128[:, :], wrsum_row[:, :])],
          reads=[b_o1, b_wsr], writes=[b_rpq])
    vop(lambda Eg: Eg.tensor_copy(wrsum_bc[:, :], rpps[:, :]), reads=[b_rpq], writes=[b_wsb])
    for jc in range(NIC):
        dma("sync", rt[:, 0:1024], wo_h[jc * 128:(jc + 1) * 128, :], writes=[b_rt])
        for cc in range(NIC):
            tok = P.emit("tensor",
                         lambda Eg, o=wwps[:, cc * E:(cc + 1) * E],
                                l=rt[:, cc * 128:(cc + 1) * 128], r=wrf[jc][:, :],
                                s=(jc == 0 and cc == 0), e=(jc == NIC - 1):
                         Eg.matmul(o, l, r, start=s, stop=e),
                         reads=[b_rt, b_wrf[jc]],
                         writes=[b_wwq] if (jc == 0 and cc == 0) else [],
                         inc=("tP", 1), bookkeep=False)
            b_rt.readers.append(tok); b_wrf[jc].readers.append(tok)
            if (jc == 0 and cc == 0) or (jc == NIC - 1 and cc == NIC - 1):
                b_wwq.w = tok
    for cc in range(NIC):
        vop(lambda Eg, cc=cc: Eg.tensor_copy(wowrb[cc][:, :], wwps[:, cc * E:(cc + 1) * E]),
            reads=[b_wwq], writes=[b_wwb[cc]])

    for jc in range(NIC):
        for tq in range(4):
            qi = (jc * 4 + tq) % 2
            sl = slice(tq * 512, (tq + 1) * 512)
            mmgrp(qps2[qi][:, :],
                  [(QT[ic][:, jc * 128:(jc + 1) * 128], XF[ic][:, sl]) for ic in range(NIC)],
                  reads=bQT + bXF, writes=[b_qps2[qi]])
            t1 = vop(lambda Eg, jc=jc, qi=qi: Eg.tensor_scalar(
                sc[0][:, :], qps2[qi][:, :], bqc[:, jc:jc + 1], 0.0, ALU.add, ALU.min),
                reads=[b_qps2[qi], b_bqc], writes=[bsc[0]])
            t2 = sop(lambda Eg: Eg.activation(out=sc[1][:, :], in_=sc[0][:, :], func=AF.Exp),
                     reads=[bsc[0]], writes=[bsc[1]])
            t3 = sop(lambda Eg, jc=jc, qi=qi: Eg.activation(
                out=sc[2][:, :], in_=qps2[qi][:, :], func=AF.Relu, bias=bqc[:, jc:jc + 1]),
                reads=[b_qps2[qi], b_bqc], writes=[bsc[2]])
            tq4 = vop(lambda Eg, jc=jc, sl=sl: Eg.tensor_add(
                QB[jc][:, sl], sc[2][:, :], sc[1][:, :]),
                reads=[bsc[1], bsc[2]], writes=[bQB[jc]] if tq == 0 else [],
                extra=() if tq == 0 else (bQB[jc].w,))
            if tq == 3:
                bQB[jc].w = tq4
    for tc in range(NTC):
        mmgrp(rpps[:, :],
              [(XF[ic][:, tc * 128:(tc + 1) * 128], wrbs[:, ic * E:(ic + 1) * E])
               for ic in range(NIC)],
              reads=bXF + [b_wrb], writes=[b_rpq])
        vop(lambda Eg, tc=tc: Eg.tensor_copy(RP[tc][:, :], rpps[:, :]),
            reads=[b_rpq], writes=[b_RP[tc]])

    idma(kv_sb[:, :], None, cc_kv_out[:, :], IndirectOffsetOnAxis(kvidx[0][:, :], 0),
         reads=[d_kvout, b_kvx[0]], writes=[b_kvs], sem="dKV")
    idma(rt[:, 0:1032], None, cc_kv_out[:, :], IndirectOffsetOnAxis(kvidx[1][:, :], 0),
         reads=[d_kvout, b_kvx[1]], writes=[b_rt], sem="dKV")
    vop(lambda Eg, rt=rt: Eg.tensor_add(kv_sb[:, :], rt[:, 0:1032], kv_sb[:, :]),
        reads=[b_rt, b_kvs], writes=[b_kvs])
    for p8_ in range(8):
        vop(lambda Eg, p8_=p8_: Eg.tensor_mul(
            kvb[:, p8_ * 128:(p8_ + 1) * 128], kv_sb[:, p8_ * 128:(p8_ + 1) * 128],
            bdmask[:, :]),
            reads=[b_kvs, b_bdm], writes=[b_kvb] if p8_ == 0 else [],
            extra=() if p8_ == 0 else (b_kvb.w,))
    b_kvb.w = ("tV", P.cnt["tV"])
    for p in range(8):
        vop(lambda Eg, p=p: Eg.tensor_scalar(
            ks2[:, 2 * p:2 * p + 2], hmask2[:, :], kv_sb[:, 1024 + p:1025 + p], None,
            ALU.mult),
            reads=[b_hm2, b_kvs], writes=[b_ks2] if p == 0 else [],
            extra=() if p == 0 else (b_ks2.w,))
    b_ks2.w = ("tV", P.cnt["tV"])

    P.barrier("tensor", P.all_tokens())
    esPB1.close()
    esPB2 = ExitStack()
    dnps2 = [esPB2.enter_context(nc.psum_tensor(f"dnps{i}", [2, 512], F32)) for i in range(2)]
    b_dnq2 = [Buf(f"dnps{i}") for i in range(2)]
    bcps = [esPB2.enter_context(nc.psum_tensor(f"bcps{i}", [128, 512], F32)) for i in range(2)]
    b_bcq = [Buf(f"bcps{i}") for i in range(2)]
    atps = [esPB2.enter_context(nc.psum_tensor(f"atps{i}", [128, 512], F32)) for i in range(2)]
    b_atq = [Buf(f"atps{i}") for i in range(2)]

    for p in range(8):
        pp = p % 2
        b_rd = b_rd2[pp]; b_rdb = b_rdb2[pp]
        rd_sb = rd_sb2[pp]; rdb_sb = rdb_sb2[pp]
        dnps = dnps2[pp]; b_dnq = b_dnq2[pp]
        for tq in range(4):
            sl = slice(tq * 512, (tq + 1) * 512)
            mmgrp(dnps[:, :], [(ks2[:, 2 * p:2 * p + 2], QB[p][:, sl])],
                  reads=[b_ks2, bQB[p]], writes=[b_dnq])
            tmax = vop(lambda Eg, sl=sl, rd_sb=rd_sb, dnps=dnps: Eg.tensor_scalar(
                rd_sb[:, sl], dnps[:, :], 1e-6, None, ALU.max),
                reads=[b_dnq], writes=[b_rd] if tq == 0 else [],
                extra=() if tq == 0 else (b_rd.w,))
            if tq == 3:
                b_rd.w = tmax
        tl_ = sop(lambda Eg, rd_sb=rd_sb: Eg.activation(out=rd_sb[:, :], in_=rd_sb[:, :],
                                                        func=AF.Ln),
                  reads=[b_rd], writes=[b_rd])
        tcb = sop(lambda Eg, rd_sb=rd_sb, rdb_sb=rdb_sb: Eg.activation(
            out=rdb_sb[:, :], in_=rd_sb[:, :], func=AF.Exp, scale=-1.0),
            reads=[b_rd], writes=[b_rdb])
        for tq in range(4):
            sl = slice(tq * 512, (tq + 1) * 512)
            pb = tq % 2
            mmgrp(bcps[pb][:, :], [(bmask[:, :], rdb_sb[:, sl])],
                  reads=[b_bm, b_rdb], writes=[b_bcq[pb]])
            tqq = vop(lambda Eg, p=p, sl=sl, pb=pb: Eg.tensor_mul(
                XF[p][:, sl], QB[p][:, sl], bcps[pb][:, :]),
                reads=[bQB[p], b_bcq[pb]], writes=[bXF[p]] if tq == 0 else [],
                extra=() if tq == 0 else (bXF[p].w,))
            if tq == 3:
                bXF[p].w = tqq
            mmgrp(atps[pb][:, :], [(kvb[:, p * 128:(p + 1) * 128], XF[p][:, sl])],
                  reads=[b_kvb, bXF[p]], writes=[b_atq[pb]], extra=(tqq,))
            tev = vop(lambda Eg, p=p, sl=sl, pb=pb: Eg.tensor_copy(
                QB[p][:, sl], atps[pb][:, :]),
                reads=[b_atq[pb]], writes=[])
            bQB[p].readers.append(tev)
            if tq == 0:
                bATB_w0 = tev
            if tq == 3:
                bQB[p].w = tev
                bQB[p].readers = []

    ATB = QB
    bATB = bQB

    P.barrier("tensor", P.all_tokens())
    esPB2.close()
    esPB3 = ExitStack()
    aops2 = [esPB3.enter_context(nc.psum_tensor(f"aops{i}", [128, C], F32)) for i in range(2)]
    b_aoq2 = [Buf(f"aops{i}") for i in range(2)]
    lgps2 = [esPB3.enter_context(nc.psum_tensor(f"lgps{i}", [128, E], F32)) for i in range(2)]
    b_lgq2 = [Buf(f"lgps{i}") for i in range(2)]
    cnps = esPB3.enter_context(nc.psum_tensor("cnps", [1, E], F32)); b_cnq = Buf("cnps")

    tz = vop(lambda Eg: Eg.memset(crow[0][:, :], 0.0), writes=[b_crw[0]])
    dma("sync", Xs[0][:, :], x_tm_h[0:128, :], writes=[bXs[0]], sem="dX")
    dma("sync", Xs[1][:, :], x_tm_h[128:256, :], writes=[bXs[1]], sem="dX")
    for tc in range(NTC):
        ai = tc % 2
        aops = aops2[ai]; b_aoq = b_aoq2[ai]
        lgps = lgps2[ai]; b_lgq = b_lgq2[ai]
        rt = rt_2[ai]; b_rt = b_rt_2[ai]
        xfb = xfb_2[ai]; b_xfb = b_xfb_2[ai]
        ln_c = lnc_2[ai]; b_lnc = b_lnc_2[ai]
        top8 = t8_2[ai]; b_t8 = b_t8_2[ai]
        tmp8a = m8a_2[ai]; b_m8a = b_m8a_2[ai]
        tmp8b = m8b_2[ai]; b_m8b = b_m8b_2[ai]
        tmp1b = m1b_2[ai]; b_m1b = b_m1b_2[ai]
        for jh in range(2):
            sl = slice(jh * 512, (jh + 1) * 512)
            mmgrp(aops[:, sl],
                  [(ATB[cc][:, tc * 128:(tc + 1) * 128], OT[cc][:, sl]) for cc in range(NIC)] +
                  [(onesb[:, 0:128], bob[:, sl])],
                  reads=bATB + bOT + [b_onesb, b_bob], writes=[b_aoq])
        mmgrp(lgps[:, :],
              [(ATB[cc][:, tc * 128:(tc + 1) * 128], wowrb[cc][:, :]) for cc in range(NIC)] +
              [(onesb[:, 0:128], bowr_bf[:, :])],
              reads=bATB + b_wwb + [b_onesb, b_bwrb], writes=[b_lgq])
        t_r = vop(lambda Eg, tc=tc, aops=aops, rt=rt, ln_c=ln_c: Eg.scalar_tensor_tensor(
            rt[:, 0:1024], aops[:, :], 1.0, Xs[tc % 2][:, :], ALU.mult, ALU.add,
            accum_out=ln_c[:, 0:1]),
            reads=[b_aoq, bXs[tc % 2]], writes=[b_rt, b_lnc])
        if tc + 2 < NTC:
            dma("sync", Xs[tc % 2][:, :], x_tm_h[(tc + 2) * 128:(tc + 3) * 128, :],
                writes=[bXs[tc % 2]], sem="dX")
        t_v = sop(lambda Eg, rt=rt, xfb=xfb, ln_c=ln_c: Eg.activation(
            out=xfb[:, :], in_=rt[:, 0:1024], func=AF.Square, accum_out=ln_c[:, 2:3]),
            reads=[b_rt], writes=[b_xfb, b_lnc])
        t_m = vop(lambda Eg, ln_c=ln_c: Eg.tensor_scalar(ln_c[:, 1:2], ln_c[:, 0:1],
                                                         1.0 / C, None, ALU.mult),
                  reads=[b_lnc], writes=[b_lnc])
        t_e = vop(lambda Eg, ln_c=ln_c: Eg.tensor_scalar(ln_c[:, 3:4], ln_c[:, 2:3],
                                                         1.0 / C, 1e-5, ALU.mult, ALU.add),
                  reads=[b_lnc], writes=[b_lnc])
        t_s1 = vop(lambda Eg, ln_c=ln_c: Eg.tensor_scalar(
            ln_c[:, 4:5], ln_c[:, 1:2], ln_c[:, 1:2], None, ALU.mult),
            reads=[b_lnc], writes=[b_lnc])
        t_s2 = vop(lambda Eg, ln_c=ln_c: Eg.tensor_sub(
            ln_c[:, 5:6], ln_c[:, 3:4], ln_c[:, 4:5]),
            reads=[b_lnc], writes=[b_lnc])
        t_q = sop(lambda Eg, ln_c=ln_c: Eg.activation(out=ln_c[:, 5:6], in_=ln_c[:, 5:6],
                                                      func=AF.Ln),
                  reads=[b_lnc], writes=[b_lnc])
        t_i = sop(lambda Eg, ln_c=ln_c: Eg.activation(out=ln_c[:, 5:6], in_=ln_c[:, 5:6],
                                                      func=AF.Exp, scale=-0.5),
                  reads=[b_lnc], writes=[b_lnc])
        t_s3 = vop(lambda Eg, ln_c=ln_c: Eg.tensor_scalar(
            ln_c[:, 6:7], ln_c[:, 1:2], ln_c[:, 5:6], None, ALU.mult),
            reads=[b_lnc], writes=[b_lnc])
        t_x = vop(lambda Eg, rt=rt, xfb=xfb, ln_c=ln_c: Eg.tensor_scalar(
            xfb[:, :], rt[:, 0:1024], ln_c[:, 5:6], ln_c[:, 6:7], ALU.mult, ALU.subtract),
            reads=[b_rt, b_lnc], writes=[b_xfb])
        dma("sync", cc_xf_in[tc * 128:(tc + 1) * 128, :], xfb[:, :],
            reads=[b_xfb], writes=[d_xfin] if tc == 0 else [],
            extra=() if tc == 0 else (d_xfin.w,), sem="dF")
        d_xfin.w = ("dF", P.cnt["dF"])
        if tc % 4 == 3 and tc < NTC - 1:
            q_ = tc // 4
            coll("AllGather", ALU.bypass,
                 cc_xf_in[q_ * 512:(q_ + 1) * 512, :],
                 cc_xf_out[q_ * 4096:(q_ + 1) * 4096, :],
                 reads=[d_xfin], writes=[d_xfout] if q_ == 0 else [],
                 extra=() if q_ == 0 else (d_xfout.w,))
        t_l1 = vop(lambda Eg, tc=tc, tmp8a=tmp8a, ln_c=ln_c: Eg.tensor_scalar(
            tmp8a[:, :], wrsum_bc[:, :], ln_c[:, 1:2], None, ALU.mult),
            reads=[b_wsb, b_lnc], writes=[b_m8a])
        t_l2 = vop(lambda Eg, tc=tc, lgps=lgps, tmp8a=tmp8a, tmp8b=tmp8b: Eg.tensor_sub(
            tmp8b[:, :], lgps[:, :], tmp8a[:, :]),
            reads=[b_lgq, b_m8a], writes=[b_m8b])
        t_l3 = vop(lambda Eg, tc=tc, tmp8a=tmp8a, tmp8b=tmp8b: Eg.tensor_add(
            tmp8a[:, :], tmp8b[:, :], RP[tc][:, :]),
            reads=[b_m8b, b_RP[tc]], writes=[b_m8a])
        t_l4 = vop(lambda Eg, tc=tc, tmp8a=tmp8a, ln_c=ln_c: Eg.tensor_scalar(
            lgs[tc][:, :], tmp8a[:, :], ln_c[:, 5:6], None, ALU.mult),
            reads=[b_m8a, b_lnc], writes=[b_lgs[tc]])
        t_rm = vop(lambda Eg, tc=tc, top8=top8: Eg.reduce_max(
            top8[:, 0:1], lgs[tc][:, :], axis=AX.X),
            reads=[b_lgs[tc]], writes=[b_t8])
        t_nm = vop(lambda Eg, tc=tc, top8=top8: Eg.tensor_scalar(
            top8[:, 1:2], top8[:, 0:1], -1.0, None, ALU.mult),
            reads=[b_t8], writes=[b_t8])
        t_oh = vop(lambda Eg, tc=tc, top8=top8: Eg.tensor_scalar(
            onehot[tc][:, :], lgs[tc][:, :], top8[:, 0:1], None, ALU.is_equal),
            reads=[b_lgs[tc], b_t8], writes=[b_oh[tc]])
        t_mu = vop(lambda Eg, tc=tc, tmp8b=tmp8b: Eg.tensor_mul(
            tmp8b[:, :], onehot[tc][:, :], iota8[:, :]),
            reads=[b_oh[tc], b_io8], writes=[b_m8b])
        tokc = P.emit("tensor",
                      lambda Eg, o=cnps[:, :], l=ones_col[:, :], r=onehot[tc][:, :],
                             s=(tc == 0), e=(tc == NTC - 1):
                      Eg.matmul(o, l, r, start=s, stop=e),
                      reads=[b_onesc, b_oh[tc]],
                      writes=[b_cnq] if tc == 0 else [],
                      inc=("tP", 1), bookkeep=False)
        b_oh[tc].readers.append(tokc)
        if tc == 0 or tc == NTC - 1:
            b_cnq.w = tokc
        t_ei = vop(lambda Eg, tc=tc, tmp8b=tmp8b: Eg.reduce_sum(
            eidx_f[tc][:, :], tmp8b[:, :], axis=AX.X),
            reads=[b_m8b], writes=[b_ei[tc]])
        t_ex = sop(lambda Eg, tc=tc, tmp8b=tmp8b, top8=top8, tmp1b=tmp1b: Eg.activation(
            out=tmp8b[:, :], in_=lgs[tc][:, :], func=AF.Exp, bias=top8[:, 1:2],
            accum_out=tmp1b[:, 0:1]),
            reads=[b_lgs[tc], b_t8], writes=[b_m8b, b_m1b])
        t_pt = vop(lambda Eg, tc=tc, tmp1b=tmp1b: Eg.reciprocal(ptok[tc][:, :], tmp1b[:, :]),
                   reads=[b_m1b], writes=[b_ptk[tc]])
        if debug_stage == 2:
            P.emit("sync", lambda Eg, tc=tc: Eg.dma_start(
                out=dbg["mr"][tc * 128:(tc + 1) * 128, 0:1], in_=eidx_f[tc][:, :],
                allow_slow_non_contiguous=True),
                reads=[b_ei[tc]], inc=("dS", 16))
            P.emit("sync", lambda Eg, tc=tc: Eg.dma_start(
                out=dbg["mr"][tc * 128:(tc + 1) * 128, 1:2], in_=ptok[tc][:, :],
                allow_slow_non_contiguous=True),
                reads=[b_ptk[tc]], inc=("dS", 16))

    if debug_stage == 2:
        for cc2 in range(NIC):
            dma("sync", dbg["wowr"][:, cc2 * E:(cc2 + 1) * E], wowrb[cc2][:, :],
                reads=[b_wwb[cc2]])
    if debug_stage == 2:
        for tcd in range(NTC):
            P.emit("sync", lambda Eg, tcd=tcd: Eg.dma_start(
                out=dbg["mr"][tcd * 128:(tcd + 1) * 128, 2:3], in_=eidx_f[tcd][:, :],
                allow_slow_non_contiguous=True),
                reads=[b_ei[tcd]], inc=("dS", 16))
    vop(lambda Eg: Eg.tensor_copy(cnt_row[:, :], cnps[:, :]), reads=[b_cnq], writes=[b_cnr])
    dma("gpsimd", cc_cnt_in[:, :], cnt_row[:, :], reads=[b_cnr], writes=[d_cntin], sem="dCNT")
    coll("AllGather", ALU.bypass, cc_cnt_in[:, :], cc_cnt_out[:, :],
         reads=[d_cntin], writes=[d_cntout])
    coll("AllGather", ALU.bypass,
         cc_xf_in[3 * 512:4 * 512, :],
         cc_xf_out[3 * 4096:4 * 4096, :],
         reads=[d_xfin], extra=(d_xfout.w,))
    d_xfout.w = ("cc", P.cnt["cc"])

    P.barrier("tensor", P.all_tokens())
    esPB3.close()
    esPR = ExitStack()
    rkps = esPR.enter_context(nc.psum_tensor("rkps", [128, E], F32)); b_rkq = Buf("rkps")
    csps = esPR.enter_context(nc.psum_tensor("csps", [1, E], F32)); b_csq = Buf("csps")
    for tc in range(NTC):
        pr = tc % 2
        mmgrp(rkps[:, :],
              [(ltri[:, :], onehot[tc][:, :]), (ones_1x128[:, :], crow[pr][:, :])],
              reads=[b_ltr, b_oh[tc], b_o1, b_crw[pr]], writes=[b_rkq])
        vop(lambda Eg, tc=tc: Eg.tensor_copy(rank_sb[tc][:, :], rkps[:, :]),
            reads=[b_rkq], writes=[b_rk[tc]])
        mmgrp(csps[:, :], [(ones_col[:, :], onehot[tc][:, :])],
              reads=[b_onesc, b_oh[tc]], writes=[b_csq])
        vop(lambda Eg: Eg.tensor_copy(ccol[:, :], csps[:, :]), reads=[b_csq], writes=[b_ccl])
        vop(lambda Eg, pr=pr: Eg.tensor_add(crow[1 - pr][:, :], crow[pr][:, :], ccol[:, :]),
            reads=[b_crw[pr], b_ccl], writes=[b_crw[1 - pr]])

    dma("gpsimd", cnt8[:, :], cc_cnt_out[:, :], reads=[d_cntout], writes=[b_cn8], sem="dCNT")
    P.barrier("tensor", P.all_tokens())
    esPR.close()
    esPC = ExitStack()
    bsps = esPC.enter_context(nc.psum_tensor("bsps", [128, E], F32)); b_bsq = Buf("bsps")
    mmgrp(bsps[0:1, :], [(ltme[:, :], cnt8[:, :])], reads=[b_ltm, b_cn8], writes=[b_bsq])
    vop(lambda Eg: Eg.tensor_copy(base_row[:, :], bsps[0:1, :]), reads=[b_bsq], writes=[b_bsr])
    mmgrp(bsps[:, :], [(ones_1x128[:, :], base_row[:, :])],
          reads=[b_o1, b_bsr], writes=[b_bsq])
    vop(lambda Eg: Eg.tensor_copy(base_bc[:, :], bsps[:, :]), reads=[b_bsq], writes=[b_bbc])

    tabps = esPC.enter_context(nc.psum_tensor("tabps", [128, 132], F32))
    b_tbq = Buf("tabps")
    vop(lambda Eg: Eg.tensor_add(
        cm8a[:, :], rank_all[:, :],
        base_bc[:, :].rearrange("p e -> p () e").broadcast_to([128, NTC, E])),
        reads=b_rk + [b_bbc], writes=[b_cm8a])
    vop(lambda Eg: Eg.tensor_mul(cm8b[:, :], cm8a[:, :], oh_all[:, :]),
        reads=[b_cm8a] + b_oh, writes=[b_cm8b])
    for tc in range(NTC):
        vop(lambda Eg, tc=tc: Eg.reduce_sum(
            gsl_all[:, tc:tc + 1], cm8b[:, tc * E:(tc + 1) * E], axis=AX.X),
            reads=[b_cm8b], writes=[b_gsla])
    vop(lambda Eg: Eg.tensor_scalar(kee_all[:, :], gsl_all[:, :], float(CAP), None,
                                    ALU.is_lt),
        reads=[b_gsla], writes=[b_keea])
    vop(lambda Eg: Eg.tensor_mul(psc_all[:, :], ptk_all[:, :], kee_all[:, :]),
        reads=b_ptk + [b_keea], writes=[b_psca])
    vop(lambda Eg: Eg.scalar_tensor_tensor(
        cm1a[:, :], ei_all[:, :], float(CAP), gsl_all[:, :], ALU.mult, ALU.add),
        reads=b_ei + [b_gsla], writes=[b_cm1a])
    vop(lambda Eg: Eg.tensor_scalar(cm1b[:, :], cm1a[:, :], trash[:, 0:1], None,
                                    ALU.subtract),
        reads=[b_cm1a, b_trh], writes=[b_cm1b])
    vop(lambda Eg: Eg.tensor_mul(cm1b[:, :], cm1b[:, :], kee_all[:, :]),
        reads=[b_cm1b, b_keea], writes=[b_cm1b])
    vop(lambda Eg: Eg.tensor_scalar(cm1c[:, :], cm1b[:, :], trash[:, 0:1], None,
                                    ALU.add),
        reads=[b_cm1b, b_trh], writes=[b_cm1c])
    vop(lambda Eg: Eg.tensor_copy(sx_all[:, :], cm1c[:, :]), reads=[b_cm1c], writes=[b_sxa])
    vop(lambda Eg: Eg.tensor_scalar(cm1a[:, :], gsl_all[:, :], 512.0, None, ALU.is_ge),
        reads=[b_gsla], writes=[b_cm1a])
    vop(lambda Eg: Eg.tensor_scalar(cm1b[:, :], gsl_all[:, :], 1024.0, None, ALU.is_ge),
        reads=[b_gsla], writes=[b_cm1b])
    vop(lambda Eg: Eg.tensor_add(cm1a[:, :], cm1a[:, :], cm1b[:, :]),
        reads=[b_cm1b, b_cm1a], writes=[b_cm1a])
    vop(lambda Eg: Eg.tensor_scalar(cm1b[:, :], gsl_all[:, :], 1536.0, None, ALU.is_ge),
        reads=[b_gsla], writes=[b_cm1b])
    vop(lambda Eg: Eg.tensor_add(cm1a[:, :], cm1a[:, :], cm1b[:, :]),
        reads=[b_cm1b, b_cm1a], writes=[b_cm1a])
    vop(lambda Eg: Eg.scalar_tensor_tensor(
        cm1b[:, :], cm1a[:, :], -512.0, gsl_all[:, :], ALU.mult, ALU.add),
        reads=[b_cm1a, b_gsla], writes=[b_cm1b])
    vop(lambda Eg: Eg.scalar_tensor_tensor(
        cm1c[:, :], cm1a[:, :], 4096.0, cm1b[:, :], ALU.mult, ALU.add),
        reads=[b_cm1a, b_cm1b], writes=[b_cm1c])
    vop(lambda Eg: Eg.scalar_tensor_tensor(
        cm1a[:, :], ei_all[:, :], 512.0, cm1c[:, :], ALU.mult, ALU.add),
        reads=b_ei + [b_cm1c], writes=[b_cm1a])
    vop(lambda Eg: Eg.tensor_mul(yrw_f[:, :], cm1a[:, :], kee_all[:, :]),
        reads=[b_cm1a, b_keea], writes=[b_yrwf])
    for tc in range(NTC):
        vop(lambda Eg, tc=tc: Eg.tensor_copy(yrow[tc][:, :], yrw_f[:, tc:tc + 1]),
            reads=[b_yrwf], writes=[b_yrw[tc]])
    vop(lambda Eg: Eg.tensor_scalar(hi_i_all[:, :], sx_all[:, :], 7, None,
                                    ALU.logical_shift_right),
        reads=[b_sxa], writes=[b_hiia])
    vop(lambda Eg: Eg.tensor_copy(hi_f_all[:, :], hi_i_all[:, :]),
        reads=[b_hiia], writes=[b_hifa])
    vop(lambda Eg: Eg.tensor_scalar(lo_i_all[:, :], sx_all[:, :], 127, None,
                                    ALU.bitwise_and),
        reads=[b_sxa], writes=[b_loia])
    vop(lambda Eg: Eg.tensor_copy(lo_f_all[:, :], lo_i_all[:, :]),
        reads=[b_loia], writes=[b_lofa])
    for tc in range(NTC):
        tA = vop(lambda Eg, tc=tc: Eg.tensor_scalar(
            Afm[:, :], iota132[:, :], hi_f_all[:, tc:tc + 1], None, ALU.is_equal),
            reads=[b_i132, b_hifa], writes=[b_Afm])
        tB = vop(lambda Eg, tc=tc: Eg.tensor_scalar(
            Bfm[:, :], iota128f[:, :], lo_f_all[:, tc:tc + 1], None, ALU.is_equal),
            reads=[b_i128, b_lofa], writes=[b_Bfm])
        tBg = vop(lambda Eg, tc=tc: Eg.tensor_scalar(Bg[:, :], Bfm[:, :],
                                                     gidf_sb[tc], None, ALU.mult),
                  reads=[b_Bfm, b_gidf[tc]], writes=[b_Bg])
        tok = P.emit("tensor",
                     lambda Eg, o=tabps[:, :], l=Bg[:, :], r=Afm[:, :],
                            s=(tc == 0), e=(tc == NTC - 1):
                     Eg.matmul(o, l, r, start=s, stop=e),
                     reads=[b_Bg, b_Afm],
                     writes=[b_tbq] if tc == 0 else [],
                     inc=("tP", 1), bookkeep=False)
        b_Bg.readers.append(tok); b_Afm.readers.append(tok)
        if tc == 0 or tc == NTC - 1:
            b_tbq.w = tok
    if debug_stage == 2:
        for tcd in range(NTC):
            P.emit("sync", lambda Eg, tcd=tcd: Eg.dma_start(
                out=dbg["mr"][tcd * 128:(tcd + 1) * 128, 3:4], in_=eidx_f[tcd][:, :],
                allow_slow_non_contiguous=True),
                reads=[b_ei[tcd]], inc=("dS", 16))
    vop(lambda Eg: Eg.tensor_copy(tabsb_f[:, :], tabps[:, :]), reads=[b_tbq], writes=[b_tbf])
    vop(lambda Eg: Eg.tensor_copy(tabsb_i[:, :], tabsb_f[:, :]), reads=[b_tbf], writes=[b_tbi])
    dma("gpsimd", table_2d, tabsb_i[:, :], reads=[b_tbi], writes=[d_tab], sem="dTAB")
    coll("AllReduce", ALU.add, table_2d, table_2d, reads=[d_tab], writes=[d_tab])

    if debug_stage == 1:
        dma("gpsimd", dbg["xln"], cc_xf_in[:, :], reads=[d_xfin], writes=[d_dbg])
    if debug_stage == 2:
        dma("gpsimd", dbg["table"], table_2d, reads=[d_tab], writes=[d_dbg],
            extra=[d_tab.w])
        dma("gpsimd", dbg["cnt"], cc_cnt_out[:, :], reads=[d_cntout], writes=[d_dbg])
        metasb = sb("metasb", [128, 4]); b_met = Buf("metasb")
        for tc in range(NTC):
            vop(lambda Eg, tc=tc: Eg.tensor_copy(metasb[:, 0:1], eidx_f[tc][:, :]),
                reads=[b_ei[tc]], writes=[b_met])
            t1 = vop(lambda Eg, tc=tc: Eg.tensor_copy(metasb[:, 1:2], gslot[tc][:, :]),
                     reads=[b_gsl[tc]], extra=(b_met.w,))
            t2 = vop(lambda Eg, tc=tc: Eg.tensor_copy(metasb[:, 2:3], keepc[tc][:, :]),
                     reads=[b_kee[tc]], extra=(t1,))
            t3 = vop(lambda Eg, tc=tc: Eg.tensor_copy(metasb[:, 3:4], ptok[tc][:, :]),
                     reads=[b_ptk[tc]], extra=(t2,))
            b_met.w = t3
            dma("sync", dbg["meta"][tc * 128:(tc + 1) * 128, :], metasb[:, :],
                reads=[b_met], sem="dDBG")
            dma("sync", dbg["lgs"][tc * 128:(tc + 1) * 128, :], lgs[tc][:, :],
                reads=[b_lgs[tc]])
            dma("sync", dbg["rp"][tc * 128:(tc + 1) * 128, :], RP[tc][:, :],
                reads=[b_RP[tc]])

    if debug_stage in (1, 2):
        P.full_barrier(exclude=())
        return nc, P

    P.full_barrier(exclude=("cc",))
    esPC.close()
    esA.close()

    esD = ExitStack()
    def sbD(name, shape, dt=F32):
        return esD.enter_context(nc.sbuf_tensor("sd_" + name, shape, dt))

    W1b = [sbD(f"W1b{dc}", [128, FF], BF16) for dc in range(NIC)]
    bW1 = [Buf(f"W1b{dc}") for dc in range(NIC)]
    W2b = [sbD(f"W2b{fc}", [128, C], BF16) for fc in range(NFC)]
    bW2 = [Buf(f"W2b{fc}") for fc in range(NFC)]
    hbt = [sbD(f"hb{fc}", [128, 512], BF16) for fc in range(NFC)]
    bhb = [Buf(f"hb{fc}") for fc in range(NFC)]
    xef = [[sbD(f"xef{i}_{dc}", [128, 512], BF16) for dc in range(NIC)] for i in range(2)]
    bxe = [[Buf(f"xef{i}_{dc}") for dc in range(NIC)] for i in range(2)]
    yebt = [sbD(f"yeb{i}", [128, C], BF16) for i in range(2)]
    bye = [Buf(f"yeb{i}") for i in range(2)]
    xg = [sbD(f"xg{i}", [128, C], BF16) for i in range(2)]
    bxg = [Buf(f"xg{i}") for i in range(2)]

    esPD = ExitStack()
    hps = [esPD.enter_context(nc.psum_tensor(f"hps{i}", [128, 512], F32)) for i in range(2)]
    bhq = [Buf(f"hps{i}") for i in range(2)]
    yps = [esPD.enter_context(nc.psum_tensor(f"yps{i}", [128, 512], F32)) for i in range(2)]
    byq = [Buf(f"yps{i}") for i in range(2)]

    for dc in range(NIC):
        dma("gpsimd", W1b[dc][:, :], w1e_h[dc * 128:(dc + 1) * 128, :], writes=[bW1[dc]])
    for fc in range(NFC):
        dma("gpsimd", W2b[fc][:, :], w2e_h[fc * 128:(fc + 1) * 128, :], writes=[bW2[fc]])

    xidx = [sbD(f"xidx{i}", [128, 1], I32) for i in range(NTC)]
    b_xi = [Buf(f"xidx{i}") for i in range(NTC)]
    xw1 = sbD("xw1", [128, NTC], I32); b_xw1 = Buf("xw1")
    xw2 = sbD("xw2", [128, NTC], I32); b_xw2 = Buf("xw2")
    xw3 = sbD("xw3", [128, NTC], I32); b_xw3 = Buf("xw3")
    for scn in range(NTC):
        idma(tabsb[scn][:, :], None, table_d[:, :], IndirectOffsetOnAxis(tbix[scn][:, :], 0),
             reads=[d_tab, b_tbix[scn]], writes=[b_tab[scn]], sem="dTAB")
    tok_tab_all = ("dTAB", P.cnt["dTAB"])
    for scn in range(NTC):
        b_tab[scn].w = tok_tab_all
    for scn in range(NTC):
        tcp = vop(lambda Eg, scn=scn: Eg.tensor_copy(xw1[:, scn:scn + 1], tabsb[scn][:, :]),
                  reads=[b_tab[scn]], writes=[b_xw1] if scn == 0 else [],
                  extra=() if scn == 0 else (b_xw1.w,))
        if scn == NTC - 1:
            b_xw1.w = tcp
    vop(lambda Eg: Eg.tensor_scalar(xw2[:, :], xw1[:, :], 2047, None, ALU.bitwise_and),
        reads=[b_xw1], writes=[b_xw2])
    vop(lambda Eg: Eg.tensor_scalar(xw2[:, :], xw2[:, :], 9, None, ALU.logical_shift_right),
        reads=[b_xw2], writes=[b_xw2])
    vop(lambda Eg: Eg.tensor_scalar(xw2[:, :], xw2[:, :], 12, None, ALU.logical_shift_left),
        reads=[b_xw2], writes=[b_xw2])
    vop(lambda Eg: Eg.tensor_scalar(xw3[:, :], xw1[:, :], 11, None, ALU.logical_shift_right),
        reads=[b_xw1], writes=[b_xw3])
    vop(lambda Eg: Eg.tensor_scalar(xw3[:, :], xw3[:, :], 9, None, ALU.logical_shift_left),
        reads=[b_xw3], writes=[b_xw3])
    vop(lambda Eg: Eg.tensor_add(xw2[:, :], xw2[:, :], xw3[:, :]),
        reads=[b_xw3, b_xw2], writes=[b_xw2])
    vop(lambda Eg: Eg.tensor_scalar(xw3[:, :], xw1[:, :], 511, None, ALU.bitwise_and),
        reads=[b_xw1], writes=[b_xw3])
    vop(lambda Eg: Eg.tensor_add(xw2[:, :], xw2[:, :], xw3[:, :]),
        reads=[b_xw3, b_xw2], writes=[b_xw2])
    for scn in range(NTC):
        vop(lambda Eg, scn=scn: Eg.tensor_copy(xidx[scn][:, :], xw2[:, scn:scn + 1]),
            reads=[b_xw2], writes=[b_xi[scn]])

    tok_store = []
    for scn in range(NTC):
        pb = scn % 2
        idma(xg[pb][:, :], None, cc_xf_out[:, :], IndirectOffsetOnAxis(xidx[scn][:, :], 0),
             reads=[d_xfout, b_xi[scn]], writes=[bxg[pb]], sem="dXB")
        ts_ = dma("gpsimd", xe_rows_d[scn * 128:(scn + 1) * 128, :], xg[pb][:, :],
                  reads=[bxg[pb]], writes=[d_xer] if scn == 0 else [],
                  extra=() if scn == 0 else (d_xer.w,), sem="dXB")
        tok_store.append(ts_)
        if scn == NTC - 1:
            d_xer.w = ("dXB", P.cnt["dXB"])

    for sq in range(4):
        pb = sq % 2
        for dc in range(NIC):
            dma_t(xef[pb][dc][:, :],
                  xe_rows_d[sq * 512:(sq + 1) * 512, dc * 128:(dc + 1) * 128],
                  reads=[], writes=[bxe[pb][dc]],
                  extra=[tok_store[4 * sq + j] for j in range(4)])
        for fc in range(NFC):
            hp = fc % 2
            dcg = fc // 4
            mmgrp(hps[hp][:, :],
                  [(W1b[dc][:, fc * 128:(fc + 1) * 128], xef[pb][dc][:, :])
                   for dc in range(NIC)],
                  reads=bW1 + bxe[pb], writes=[bhq[hp]])
            sop(lambda Eg, fc=fc, hp=hp: Eg.activation(
                out=hbt[fc][:, :], in_=hps[hp][:, :], func=AF.Relu),
                reads=[bhq[hp]], writes=[bhb[fc]])
        for sci in range(4):
            scn = sq * 4 + sci
            yb = sci % 2
            for jh in range(2):
                sl = slice(jh * 512, (jh + 1) * 512)
                mmgrp(yps[yb][:, :],
                      [(hbt[fc][:, sci * 128:(sci + 1) * 128], W2b[fc][:, sl])
                       for fc in range(NFC)],
                      reads=bhb + bW2, writes=[byq[yb]])
                vop(lambda Eg, yb=yb, sl=sl: Eg.tensor_copy(yebt[yb][:, sl], yps[yb][:, :]),
                    reads=[byq[yb]], writes=[bye[yb]] if jh == 0 else [],
                    extra=() if jh == 0 else (bye[yb].w,))
            bye[yb].w = ("tV", P.cnt["tV"])
            dma("sync", cc_ye_in[scn * 128:(scn + 1) * 128, :], yebt[yb][:, :],
                reads=[bye[yb]], writes=[d_yein] if scn == 0 else [],
                extra=() if scn == 0 else (d_yein.w,), sem="dYE")
            d_yein.w = ("dYE", P.cnt["dYE"])
        coll("AllGather", ALU.bypass,
             cc_ye_in[sq * 512:(sq + 1) * 512, :],
             cc_ye_out[sq * 4096:(sq + 1) * 4096, :],
             reads=[d_yein], writes=[d_yeout] if sq == 0 else [],
             extra=() if sq == 0 else (d_yeout.w,))
        if sq == 3:
            d_yeout.w = ("cc", P.cnt["cc"])

    if debug_stage == 3:
        dma("gpsimd", dbg["ye"], cc_ye_in[:, :], reads=[d_yein], writes=[d_dbg])
        P.full_barrier(exclude=())
        return nc, P

    P.full_barrier(exclude=("cc",))
    esPD.close()
    esD.close()

    esE = ExitStack()
    def sbE(name, shape, dt=F32):
        return esE.enter_context(nc.sbuf_tensor("se_" + name, shape, dt))
    ygt = [sbE(f"ygt{i}", [128, C], BF16) for i in range(2)]
    byg = [Buf(f"ygt{i}") for i in range(2)]
    xft = [sbE(f"xft{i}", [128, C], BF16) for i in range(2)]
    bxf2 = [Buf(f"xft{i}") for i in range(2)]
    rtE2 = [sbE(f"rtE{i}", [128, C]) for i in range(2)]
    b_rtE2 = [Buf(f"rtE{i}") for i in range(2)]
    sqE2 = [sbE(f"sqE{i}", [128, C]) for i in range(2)]
    b_sqE2 = [Buf(f"sqE{i}") for i in range(2)]
    lnE2 = [sbE(f"lnE{i}", [128, 8]) for i in range(2)]
    b_lnE2 = [Buf(f"lnE{i}") for i in range(2)]
    outE = [sbE(f"outE{i}", [128, C]) for i in range(2)]
    bout = [Buf(f"outE{i}") for i in range(2)]

    for tc in range(NTC):
        pb = tc % 2
        rtE = rtE2[pb]; b_rtE = b_rtE2[pb]
        sqE = sqE2[pb]; b_sqE = b_sqE2[pb]
        lnE = lnE2[pb]; b_lnE = b_lnE2[pb]
        idma(ygt[pb][:, :], None, cc_ye_out[:, :], IndirectOffsetOnAxis(yrow[tc][:, :], 0),
             reads=[d_yeout, b_yrw[tc]], writes=[byg[pb]], sem="dE")
        idma(xft[pb][:, :], None, cc_xf_out[:, :], IndirectOffsetOnAxis(xfr_sb[tc][:, :], 0),
             reads=[d_xfout, b_xfr[tc]], writes=[bxf2[pb]], sem="dE")
        t_r = vop(lambda Eg, pb=pb, tc=tc, rtE=rtE, lnE=lnE: Eg.scalar_tensor_tensor(
            rtE[:, :], ygt[pb][:, :], pscale[tc][:, 0:1], xft[pb][:, :],
            ALU.mult, ALU.add, accum_out=lnE[:, 0:1]),
            reads=[byg[pb], bxf2[pb], b_psc[tc]], writes=[b_rtE, b_lnE])
        t_v = sop(lambda Eg, rtE=rtE, sqE=sqE, lnE=lnE: Eg.activation(
            out=sqE[:, :], in_=rtE[:, :], func=AF.Square, accum_out=lnE[:, 2:3]),
            reads=[b_rtE], writes=[b_sqE, b_lnE])
        t_m = vop(lambda Eg, lnE=lnE: Eg.tensor_scalar(lnE[:, 1:2], lnE[:, 0:1], 1.0 / C,
                                                       None, ALU.mult),
                  reads=[b_lnE], writes=[b_lnE])
        t_e = vop(lambda Eg, lnE=lnE: Eg.tensor_scalar(lnE[:, 3:4], lnE[:, 2:3], 1.0 / C,
                                                       1e-5, ALU.mult, ALU.add),
                  reads=[b_lnE], writes=[b_lnE])
        t_s1 = vop(lambda Eg, lnE=lnE: Eg.tensor_scalar(
            lnE[:, 4:5], lnE[:, 1:2], lnE[:, 1:2], None, ALU.mult),
            reads=[b_lnE], writes=[b_lnE])
        t_s2 = vop(lambda Eg, lnE=lnE: Eg.tensor_sub(
            lnE[:, 5:6], lnE[:, 3:4], lnE[:, 4:5]),
            reads=[b_lnE], writes=[b_lnE])
        t_q = sop(lambda Eg, lnE=lnE: Eg.activation(out=lnE[:, 5:6], in_=lnE[:, 5:6],
                                                    func=AF.Ln),
                  reads=[b_lnE], writes=[b_lnE])
        t_i = sop(lambda Eg, lnE=lnE: Eg.activation(out=lnE[:, 5:6], in_=lnE[:, 5:6],
                                                    func=AF.Exp, scale=-0.5),
                  reads=[b_lnE], writes=[b_lnE])
        t_s3 = vop(lambda Eg, lnE=lnE: Eg.tensor_scalar(
            lnE[:, 6:7], lnE[:, 1:2], lnE[:, 5:6], None, ALU.mult),
            reads=[b_lnE], writes=[b_lnE])
        t_x = vop(lambda Eg, pb=pb, rtE=rtE, lnE=lnE: Eg.tensor_scalar(
            outE[pb][:, :], rtE[:, :], lnE[:, 5:6], lnE[:, 6:7], ALU.mult, ALU.subtract),
            reads=[b_rtE, b_lnE], writes=[bout[pb]])
        dma("sync", out_h[tc * 128:(tc + 1) * 128, :], outE[pb][:, :],
            reads=[bout[pb]], writes=[d_out] if tc == 0 else [],
            extra=() if tc == 0 else (d_out.w,), sem="dO")
        if tc == NTC - 1:
            d_out.w = ("dO", P.cnt["dO"])

    P.full_barrier(exclude=())
    return nc, P


_NC_CACHE = {}


def get_nc(debug_stage=DEBUG_STAGE):
    if debug_stage not in _NC_CACHE:
        nc, P = build_nc(debug_stage)
        sems = {}
        names = sorted(P.cnt.keys())
        with ExitStack() as sem_es:
            for n in names:
                sems[n] = sem_es.enter_context(nc.semaphore(n))
            with nc.Block() as block:
                @block.sync
                def _(sync):
                    P.replay("sync", sync, sems)
                @block.tensor
                def _(tensor):
                    P.replay("tensor", tensor, sems)
                @block.vector
                def _(vector):
                    P.replay("vector", vector, sems)
                @block.scalar
                def _(scalar):
                    P.replay("scalar", scalar, sems)
                @block.gpsimd
                def _(gpsimd):
                    P.replay("gpsimd", gpsimd, sems)
        _NC_CACHE[debug_stage] = nc
    return _NC_CACHE[debug_stage]


def make_in_maps(inputs):
    x = np.ascontiguousarray(np.asarray(inputs["L0_attn_x"], dtype=np.float32).reshape(-1, C))
    Wq = np.asarray(inputs["L0_attn_Wq"], dtype=np.float32)
    Wk = np.asarray(inputs["L0_attn_Wk"], dtype=np.float32)
    Wv = np.asarray(inputs["L0_attn_Wv"], dtype=np.float32)
    Wo = np.asarray(inputs["L0_attn_Wo"], dtype=np.float32)
    bq = np.asarray(inputs["L0_attn_bq"], dtype=np.float32)
    bk = np.asarray(inputs["L0_attn_bk"], dtype=np.float32)
    bv = np.asarray(inputs["L0_attn_bv"], dtype=np.float32)
    bo = np.asarray(inputs["L0_attn_bo"], dtype=np.float32)
    Wr = np.ascontiguousarray(np.asarray(inputs["L0_ffn_Wr"], dtype=np.float32))
    W1 = np.asarray(inputs["L0_ffn_W1"], dtype=np.float32)
    W2 = np.asarray(inputs["L0_ffn_W2"], dtype=np.float32)

    import ml_dtypes
    wqT = np.ascontiguousarray(Wq.T); wkT = np.ascontiguousarray(Wk.T)
    wvT = np.ascontiguousarray(Wv.T); woT = np.ascontiguousarray(Wo.T)
    wo = np.ascontiguousarray(Wo)
    bq_cols = np.ascontiguousarray(bq.reshape(NIC, 128).T)
    bo_cols = np.ascontiguousarray(bo.reshape(NIC, 128).T)
    bk_row = np.ascontiguousarray(bk.reshape(1, C))
    bv_row = np.ascontiguousarray(bv.reshape(1, C))
    bo_row = np.ascontiguousarray(bo.reshape(1, C))
    p = np.arange(128)
    ltri = (p[:, None] < p[None, :]).astype(np.float32)
    onesb_row = np.ones((1, 128), dtype=ml_dtypes.bfloat16)
    onesb_col = np.ones((128, 1), dtype=ml_dtypes.bfloat16)
    ones_col = np.ones((128, 1), dtype=np.float32)
    ones_1x128 = np.ones((1, 128), dtype=np.float32)
    iota8 = np.tile(np.arange(E, dtype=np.float32), (128, 1))
    trash = (16384.0 + p[:, None]).astype(np.float32)
    bdmask = ((p[:, None] < 64) == (p[None, :] < 64)).astype(np.float32)
    hmask2 = np.stack([(p < 64).astype(np.float32), (p >= 64).astype(np.float32)], axis=1)
    bmask = np.zeros((2, 128), dtype=ml_dtypes.bfloat16)
    bmask[0, :64] = 1; bmask[1, 64:] = 1
    iota132c = np.tile(np.arange(132, dtype=np.float32), (128, 1))
    iota128f = np.tile(np.arange(128, dtype=np.float32), (128, 1))

    in_maps = []
    for c in range(NCORE):
        sl = slice(c * TPC, (c + 1) * TPC)
        x_tm = np.ascontiguousarray(x[sl])
        x_fm = np.ascontiguousarray(x_tm.T)
        gid = (c * TPC + np.arange(TPC, dtype=np.int32)).reshape(TPC, 1)
        _l = np.arange(TPC, dtype=np.int32)
        xfrow = ((_l >> 9) * 4096 + c * 512 + (_l & 511)).astype(np.int32).reshape(TPC, 1)
        gidf = gid.astype(np.float32)
        tabidx = ((gid & 127) * 132 + (gid >> 7)).astype(np.int32)
        pair = c // 2
        kvidx = np.concatenate([
            (2 * pair) * 128 + np.arange(128, dtype=np.int32),
            (2 * pair + 1) * 128 + np.arange(128, dtype=np.int32)]).reshape(256, 1)
        ltme = (np.arange(NCORE) < c).astype(np.float32).reshape(NCORE, 1)
        in_maps.append({
            "x_tm": x_tm, "x_fm": x_fm,
            "wqT": wqT, "wkT": wkT, "wvT": wvT, "woT": woT, "wo": wo,
            "bq_cols": bq_cols, "bo_cols": bo_cols,
            "bk_row": bk_row, "bv_row": bv_row, "bo_row": bo_row,
            "wr": Wr,
            "w1e": np.ascontiguousarray(W1[c]),
            "w2e": np.ascontiguousarray(W2[c]),
            "gid": gid, "gidf": gidf, "xfrow": xfrow, "tabidx": tabidx, "kvidx": kvidx, "ltme": ltme,
            "ltri": ltri, "onesb_row": onesb_row, "onesb_col": onesb_col,
            "ones_col": ones_col, "ones_1x128": ones_1x128,
            "iota8": iota8, "trash": trash, "bdmask": bdmask,
            "hmask2": hmask2, "bmask": bmask,
            "iota132": iota132c, "iota128f": iota128f,
        })
    return in_maps


def kernel(**inputs):
    nc = get_nc(DEBUG_STAGE)
    in_maps = make_in_maps(inputs)
    res = run_bass_kernel_spmd(nc, in_maps, core_ids=list(range(NCORE)))
    out = np.concatenate([r["y"] for r in res.results], axis=0)
    return out.reshape(B, T, C)


# version 32
# speedup vs baseline: 1.2021x; 1.2021x over previous
import numpy as np
from contextlib import ExitStack

import concourse.bass as bass
import concourse.mybir as mybir
from concourse.bass import IndirectOffsetOnAxis
from concourse.bass_utils import run_bass_kernel_spmd

F32 = mybir.dt.float32
BF16 = mybir.dt.bfloat16
I32 = mybir.dt.int32
U32 = mybir.dt.uint32
AF = mybir.ActivationFunctionType
ALU = mybir.AluOpType
AX = mybir.AxisListType

B, T, C = 4, 4096, 1024
E, CAP = 8, 2048
NCORE = 8
TPC = 2048
NTC = TPC // 128
NIC = C // 128
FF = 4096
NFC = FF // 128
TBL = 16896

DEBUG_STAGE = 0


class Buf:
    __slots__ = ("name", "w", "readers")

    def __init__(self, name):
        self.name = name
        self.w = None
        self.readers = []


class Prog:
    ENGINES = ("sync", "tensor", "vector", "scalar", "gpsimd")

    def __init__(self):
        self.q = {e: [] for e in self.ENGINES}
        self.cnt = {}

    def _tok(self, inc):
        sem, amt = inc
        self.cnt[sem] = self.cnt.get(sem, 0) + amt
        return (sem, self.cnt[sem])

    def emit(self, eng, builder, reads=(), writes=(), extra=(), inc=None,
             bookkeep=True):
        waits = set()
        for b in reads:
            if b.w is not None:
                waits.add(b.w)
        for b in writes:
            if b.w is not None:
                waits.add(b.w)
            for r in b.readers:
                waits.add(r)
        for t in extra:
            if t is not None:
                waits.add(t)
        tok = self._tok(inc) if inc is not None else None
        self.q[eng].append((builder, tuple(sorted(waits)), inc))
        if tok is not None and bookkeep:
            for b in reads:
                b.readers.append(tok)
            for b in writes:
                b.w = tok
                b.readers = []
        return tok

    def barrier(self, eng, toks):
        toks = tuple(t for t in toks if t is not None)
        if toks:
            self.q[eng].append((None, toks, None))

    def all_tokens(self, exclude=("cc",)):
        return [(s, v) for s, v in sorted(self.cnt.items()) if s not in exclude]

    def full_barrier(self, exclude=("cc",)):
        toks = self.all_tokens(exclude)
        for e in self.ENGINES:
            self.barrier(e, toks)

    def replay(self, eng_name, eng_obj, sems):
        for builder, waits, inc in self.q[eng_name]:
            for semname, val in waits:
                eng_obj.wait_ge(sems[semname], val)
            if builder is None:
                continue
            ins = builder(eng_obj)
            if inc is not None:
                sem, amt = inc
                ins.then_inc(sems[sem], amt)


def build_nc(debug_stage=DEBUG_STAGE):
    nc = bass.Bass("TRN2", target_bir_lowering=False, debug=False)
    P = Prog()

    def inp(name, shape, dt=F32):
        return nc.dram_tensor(name, shape, dt, kind="ExternalInput").ap()

    x_tm_h = inp("x_tm", [TPC, C])
    x_fm_h = inp("x_fm", [C, TPC])
    wqT_h = inp("wqT", [C, C]); wkT_h = inp("wkT", [C, C])
    wvT_h = inp("wvT", [C, C]); woT_h = inp("woT", [C, C])
    wo_h = inp("wo", [C, C])
    bq_cols_h = inp("bq_cols", [128, NIC])
    bo_cols_h = inp("bo_cols", [128, NIC])
    bk_row_h = inp("bk_row", [1, C]); bv_row_h = inp("bv_row", [1, C])
    bo_row_h = inp("bo_row", [1, C])
    wr_h = inp("wr", [C, E])
    w1e_h = inp("w1e", [C, FF]); w2e_h = inp("w2e", [FF, C])
    gid_h = inp("gid", [TPC, 1], I32)
    xfrow_h = inp("xfrow", [TPC, 1], I32)
    kvidx_h = inp("kvidx", [256, 1], I32)
    ltri_h = inp("ltri", [128, 128])
    onesb_row_h = inp("onesb_row", [1, 128], BF16)
    onesb_col_h = inp("onesb_col", [128, 1], BF16)
    ones_col_h = inp("ones_col", [128, 1])
    ones_1x128_h = inp("ones_1x128", [1, 128])
    iota8_h = inp("iota8", [128, E])
    trash_h = inp("trash", [128, 1])
    bdmask_h = inp("bdmask", [128, 128])
    hmask2_h = inp("hmask2", [128, 2])
    bmask_h = inp("bmask", [2, 128], BF16)
    ltme_h = inp("ltme", [NCORE, 1])
    iota132_h = inp("iota132", [128, 132])
    iota128f_h = inp("iota128f", [128, 128])
    gidf_h = inp("gidf", [TPC, 1])
    tabidx_h = inp("tabidx", [TPC, 1], I32)

    out_h = nc.dram_tensor("y", [TPC, C], F32, kind="ExternalOutput").ap()
    dbg = {}
    if debug_stage == 1:
        dbg["xln"] = nc.dram_tensor("dbg_xln", [TPC, C], F32, kind="ExternalOutput").ap()
    if debug_stage == 2:
        dbg["meta"] = nc.dram_tensor("dbg_meta", [TPC, 4], F32, kind="ExternalOutput").ap()
        dbg["table"] = nc.dram_tensor("dbg_table", [128, TBL // 128], I32, kind="ExternalOutput").ap()
        dbg["cnt"] = nc.dram_tensor("dbg_cnt", [NCORE, E], F32, kind="ExternalOutput").ap()
        dbg["lgs"] = nc.dram_tensor("dbg_lgs", [TPC, E], F32, kind="ExternalOutput").ap()
        dbg["rp"] = nc.dram_tensor("dbg_rp", [TPC, E], F32, kind="ExternalOutput").ap()
        dbg["mr"] = nc.dram_tensor("dbg_mr", [TPC, 4], F32, kind="ExternalOutput").ap()
        dbg["wowr"] = nc.dram_tensor("dbg_wowr", [128, E * NIC], BF16, kind="ExternalOutput").ap()
    if debug_stage == 3:
        dbg["ye"] = nc.dram_tensor("dbg_ye", [TPC, C], F32, kind="ExternalOutput").ap()

    cc_kv_in = nc.dram_tensor("cc_kv_in", [128, 1032], F32).ap()
    cc_kv_out = nc.dram_tensor("cc_kv_out", [128 * NCORE, 1032], F32).ap()
    cc_xf_in = nc.dram_tensor("cc_xf_in", [TPC, C], BF16).ap()
    cc_xf_out = nc.dram_tensor("cc_xf_out", [TPC * NCORE, C], BF16).ap()
    cc_cnt_in = nc.dram_tensor("cc_cnt_in", [1, E], F32).ap()
    cc_cnt_out = nc.dram_tensor("cc_cnt_out", [NCORE, E], F32).ap()
    table_d = nc.dram_tensor("table_d", [TBL, 1], I32).ap()
    table_2d = table_d.rearrange("(a b) 1 -> a b", a=128)
    xe_rows_d = nc.dram_tensor("xe_rows_d", [TPC, C], BF16).ap()
    cc_ye_in = nc.dram_tensor("cc_ye_in", [TPC, C], BF16).ap()
    cc_ye_out = nc.dram_tensor("cc_ye_out", [TPC * NCORE, C], BF16).ap()

    d_kvin = Buf("d_kvin"); d_kvout = Buf("d_kvout")
    d_xfin = Buf("d_xfin"); d_xfout = Buf("d_xfout")
    d_cntin = Buf("d_cntin"); d_cntout = Buf("d_cntout")
    d_tab = Buf("d_tab")
    d_xer = Buf("d_xer"); d_yein = Buf("d_yein"); d_yeout = Buf("d_yeout")
    d_out = Buf("d_out"); d_dbg = Buf("d_dbg")

    def dma(eng, out, in_, reads=(), writes=(), extra=(), sem=None):
        if sem is None:
            sem = "dS" if eng == "sync" else "dG"
        return P.emit(eng, lambda Eg, o=out, i=in_: Eg.dma_start(out=o, in_=i),
                      reads=reads, writes=writes, extra=extra, inc=(sem, 16))

    def dma_t(out, in_, reads=(), writes=(), extra=()):
        return P.emit("scalar",
                      lambda Eg, o=out, i=in_: Eg.dma_start(out=o, in_=i, transpose=True),
                      reads=reads, writes=writes, extra=extra, inc=("dT", 16))

    def idma(out, out_off, in_, in_off, reads=(), writes=(), extra=(), sem="dG"):
        return P.emit("gpsimd",
                      lambda Eg, o=out, oo=out_off, i=in_, io=in_off:
                      Eg.indirect_dma_start(o, oo, i, io),
                      reads=reads, writes=writes, extra=extra, inc=(sem, 16))

    def coll(kind, op, ins_ap, outs_ap, reads=(), writes=(), extra=()):
        return P.emit("gpsimd",
                      lambda Eg, k=kind, o=op, ia=ins_ap, oa=outs_ap:
                      Eg.collective_compute(k, o, replica_groups=[list(range(NCORE))],
                                            ins=[ia], outs=[oa]),
                      reads=reads, writes=writes, extra=extra, inc=("cc", 1))

    def mmgrp(out_ap, parts, reads=(), writes=(), extra=()):
        n = len(parts)
        tok = None
        for i, (l, r) in enumerate(parts):
            tok = P.emit("tensor",
                         lambda Eg, o=out_ap, l=l, r=r, s=(i == 0), e=(i == n - 1):
                         Eg.matmul(o, l, r, start=s, stop=e),
                         reads=reads if i == 0 else (),
                         writes=writes if i == 0 else (),
                         extra=extra if i == 0 else (),
                         inc=("tP", 1) if i == n - 1 else None,
                         bookkeep=False)
        for b in reads:
            b.readers.append(tok)
        for b in writes:
            b.w = tok
            b.readers = []
        return tok

    def vop(fn, reads=(), writes=(), extra=()):
        return P.emit("vector", fn, reads=reads, writes=writes, extra=extra, inc=("tV", 1))

    def sop(fn, reads=(), writes=(), extra=()):
        return P.emit("scalar", fn, reads=reads, writes=writes, extra=extra, inc=("tA", 1))

    es = ExitStack()
    def sb(name, shape, dt=F32):
        return es.enter_context(nc.sbuf_tensor("s_" + name, shape, dt))

    lgs = [sb(f"lgs{i}", [128, E]) for i in range(NTC)]; b_lgs = [Buf(f"lgs{i}") for i in range(NTC)]
    oh_all = sb("oh_all", [128, NTC * E])
    onehot = [oh_all[:, i * E:(i + 1) * E] for i in range(NTC)]
    b_oh = [Buf(f"oh{i}") for i in range(NTC)]
    rank_all = sb("rank_all", [128, NTC * E])
    rank_sb = [rank_all[:, i * E:(i + 1) * E] for i in range(NTC)]
    b_rk = [Buf(f"rk{i}") for i in range(NTC)]
    ei_all = sb("ei_all", [128, NTC])
    eidx_f = [ei_all[:, i:i + 1] for i in range(NTC)]
    b_ei = [Buf(f"ei{i}") for i in range(NTC)]
    ptk_all = sb("ptk_all", [128, NTC])
    ptok = [ptk_all[:, i:i + 1] for i in range(NTC)]
    b_ptk = [Buf(f"ptk{i}") for i in range(NTC)]
    psc_all = sb("psc_all", [128, NTC]); b_psca = Buf("psc_all")
    pscale = [psc_all[:, i:i + 1] for i in range(NTC)]
    b_psc = [b_psca] * NTC
    gsl_all = sb("gsl_all", [128, NTC]); b_gsla = Buf("gsl_all")
    gslot = [gsl_all[:, i:i + 1] for i in range(NTC)]
    b_gsl = [b_gsla] * NTC
    kee_all = sb("kee_all", [128, NTC]); b_keea = Buf("kee_all")
    keepc = [kee_all[:, i:i + 1] for i in range(NTC)]
    b_kee = [b_keea] * NTC
    sx_all = sb("sx_all", [128, NTC], I32); b_sxa = Buf("sx_all")
    yrw_f = sb("yrw_f", [128, NTC]); b_yrwf = Buf("yrw_f")
    hi_i_all = sb("hi_i_all", [128, NTC], I32); b_hiia = Buf("hi_i_all")
    lo_i_all = sb("lo_i_all", [128, NTC], I32); b_loia = Buf("lo_i_all")
    hi_f_all = sb("hi_f_all", [128, NTC]); b_hifa = Buf("hi_f_all")
    lo_f_all = sb("lo_f_all", [128, NTC]); b_lofa = Buf("lo_f_all")
    cm8a = sb("cm8a", [128, NTC * E]); b_cm8a = Buf("cm8a")
    cm8b = sb("cm8b", [128, NTC * E]); b_cm8b = Buf("cm8b")
    cm1a = sb("cm1a", [128, NTC]); b_cm1a = Buf("cm1a")
    cm1b = sb("cm1b", [128, NTC]); b_cm1b = Buf("cm1b")
    cm1c = sb("cm1c", [128, NTC]); b_cm1c = Buf("cm1c")
    yrow = [sb(f"yrw{i}", [128, 1], I32) for i in range(NTC)]; b_yrw = [Buf(f"yrw{i}") for i in range(NTC)]
    gid_sb = [sb(f"gid{i}", [128, 1], I32) for i in range(NTC)]; b_gid = [Buf(f"gid{i}") for i in range(NTC)]
    xfr_sb = [sb(f"xfr{i}", [128, 1], I32) for i in range(NTC)]
    b_xfr = [Buf(f"xfr{i}") for i in range(NTC)]
    crow = [sb(f"crw{i}", [1, E]) for i in range(2)]; b_crw = [Buf(f"crw{i}") for i in range(2)]
    ccol = sb("ccol", [1, E]); b_ccl = Buf("ccol")
    cnt_row = sb("cnt_row", [1, E]); b_cnr = Buf("cnt_row")
    cnt8 = sb("cnt8", [NCORE, E]); b_cn8 = Buf("cnt8")
    base_bc = sb("base_bc", [128, E]); b_bbc = Buf("base_bc")
    base_row = sb("base_row", [1, E]); b_bsr = Buf("base_row")
    wrsum_row = sb("wrsum_row", [1, E]); b_wsr = Buf("wrsum_row")
    wrsum_bc = sb("wrsum_bc", [128, E]); b_wsb = Buf("wrsum_bc")
    bowr_row = sb("bowr_row", [1, E]); b_bwr = Buf("bowr_row")
    wowrb = [sb(f"wowrb{j}", [128, E], BF16) for j in range(NIC)]
    b_wwb = [Buf(f"wowrb{j}") for j in range(NIC)]
    RP = [sb(f"RP{i}", [128, E]) for i in range(NTC)]; b_RP = [Buf(f"RP{i}") for i in range(NTC)]
    top8 = sb("top8", [128, E]); b_t8 = Buf("top8")
    idx8 = sb("idx8", [128, E], U32); b_i8 = Buf("idx8")
    tmp8a = sb("tmp8a", [128, E]); b_m8a = Buf("tmp8a")
    tmp8b = sb("tmp8b", [128, E]); b_m8b = Buf("tmp8b")
    tmp1a = sb("tmp1a", [128, 1]); b_m1a = Buf("tmp1a")
    tmp1b = sb("tmp1b", [128, 1]); b_m1b = Buf("tmp1b")
    tmp1c = sb("tmp1c", [128, 1]); b_m1c = Buf("tmp1c")
    ln_c = sb("ln_c", [128, 4]); b_lnc = Buf("ln_c")
    tbix = [sb(f"tbix{i}", [128, 1], I32) for i in range(NTC)]
    b_tbix = [Buf(f"tbix{i}") for i in range(NTC)]
    tabsb = [sb(f"tab{i}", [128, 1], I32) for i in range(NTC)]
    b_tab = [Buf(f"tab{i}") for i in range(NTC)]
    bqc = sb("bqc", [128, NIC]); b_bqc = Buf("bqc")
    boc = sb("boc", [128, NIC]); b_boc = Buf("boc")
    bkb = sb("bkb", [1, C], BF16); b_bkb = Buf("bkb")
    bvb = sb("bvb", [1, C], BF16); b_bvb = Buf("bvb")
    bob = sb("bob", [1, C], BF16); b_bob = Buf("bob")
    onesb = sb("onesb", [1, 128], BF16); b_onesb = Buf("onesb")
    onesb_col = sb("onesb_col", [128, 1], BF16); b_onesbc = Buf("onesb_col")
    ones_col = sb("ones_col", [128, 1]); b_onesc = Buf("ones_col")
    ones_1x128 = sb("ones_1x128", [1, 128]); b_o1 = Buf("ones_1x128")
    hmask2 = sb("hmask2", [128, 2]); b_hm2 = Buf("hmask2")
    bmask = sb("bmask", [2, 128], BF16); b_bm = Buf("bmask")
    ltri = sb("ltri", [128, 128]); b_ltr = Buf("ltri")
    iota8 = sb("iota8", [128, E]); b_io8 = Buf("iota8")
    trash = sb("trash", [128, 1]); b_trh = Buf("trash")
    ltme = sb("ltme", [NCORE, 1]); b_ltm = Buf("ltme")
    wrf = [sb(f"wrf{j}", [128, E]) for j in range(NIC)]
    b_wrf = [Buf(f"wrf{j}") for j in range(NIC)]
    wrbs = sb("wrbs", [128, E * NIC], BF16); b_wrb = Buf("wrbs")
    bowr_bf = sb("bowr_bf", [1, E], BF16); b_bwrb = Buf("bowr_bf")
    kvidx = [sb(f"kvix{i}", [128, 1], I32) for i in range(2)]
    b_kvx = [Buf(f"kvix{i}") for i in range(2)]

    esA = ExitStack()
    def sbA(name, shape, dt=F32):
        return esA.enter_context(nc.sbuf_tensor("sa_" + name, shape, dt))

    Xs = [sbA(f"X{i}", [128, C]) for i in range(2)]; bXs = [Buf(f"X{i}") for i in range(2)]
    XF = [sbA(f"XF{j}", [128, TPC], BF16) for j in range(NIC)]
    bXF = [Buf(f"XF{j}") for j in range(NIC)]
    QT = [sbA(f"QT{j}", [128, C], BF16) for j in range(NIC)]; bQT = [Buf(f"QT{j}") for j in range(NIC)]
    KT = [sbA(f"KT{j}", [128, C], BF16) for j in range(NIC)]; bKT = [Buf(f"KT{j}") for j in range(NIC)]
    VT = [sbA(f"VT{j}", [128, C], BF16) for j in range(NIC)]; bVT = [Buf(f"VT{j}") for j in range(NIC)]
    OT = [sbA(f"OT{j}", [128, C], BF16) for j in range(NIC)]; bOT = [Buf(f"OT{j}") for j in range(NIC)]
    QB = [sbA(f"QB{p}", [128, TPC], BF16) for p in range(8)]; bQB = [Buf(f"QB{p}") for p in range(8)]
    kbt = [sbA(f"kbt{i}", [128, C], BF16) for i in range(2)]; b_kb = [Buf(f"kbt{i}") for i in range(2)]
    vbt = [sbA(f"vbt{i}", [128, C], BF16) for i in range(2)]; b_vb = [Buf(f"vbt{i}") for i in range(2)]
    sc = [sbA(f"sc{i}", [128, 512]) for i in range(3)]; bsc = [Buf(f"sc{i}") for i in range(3)]
    sc3b = sbA("sc3b", [128, 512])
    bsc2 = [[Buf(f"sc2{j}_{i}") for i in range(3)] for j in range(2)]
    bsc2[0] = bsc
    rt_2 = [sbA("rt", [128, 1040]), None]
    b_rt_2 = [Buf("rt0"), Buf("rt1")]
    rt = rt_2[0]; b_rt = b_rt_2[0]
    rt_2[1] = sbA("rt1", [128, C])
    sc2sets = [sc, [rt_2[1][:, 0:512], rt_2[1][:, 512:1024], sc3b]]
    xfb_2 = [sbA(f"xfb{i}", [128, C], BF16) for i in range(2)]
    b_xfb_2 = [Buf(f"xfb{i}") for i in range(2)]
    lnc_2 = [sbA(f"lnc2_{i}", [128, 8]) for i in range(2)]
    b_lnc_2 = [Buf(f"lnc2_{i}") for i in range(2)]
    t8_2 = [sbA(f"t8_{i}", [128, E]) for i in range(2)]
    b_t8_2 = [Buf(f"t8_{i}") for i in range(2)]
    m8a_2 = [sbA(f"m8a_{i}", [128, E]) for i in range(2)]
    b_m8a_2 = [Buf(f"m8a_{i}") for i in range(2)]
    m8b_2 = [sbA(f"m8b_{i}", [128, E]) for i in range(2)]
    b_m8b_2 = [Buf(f"m8b_{i}") for i in range(2)]
    m1b_2 = [sbA(f"m1b_{i}", [128, 1]) for i in range(2)]
    b_m1b_2 = [Buf(f"m1b_{i}") for i in range(2)]
    kv_sb = sbA("kv_sb", [128, 1032]); b_kvs = Buf("kv_sb")
    kvb = sbA("kvb", [128, C], BF16); b_kvb = Buf("kvb")
    bdmask = sbA("bdmask", [128, 128]); b_bdm = Buf("bdmask")
    ks2 = sbA("ks2", [128, 16], BF16); b_ks2 = Buf("ks2")
    rd_sb4 = sbA("rd_sb4", [34, TPC])
    rd_sb2 = [rd_sb4[0:2, :], rd_sb4[32:34, :]]
    b_rd2 = [Buf(f"rd_sb{i}") for i in range(2)]
    rdb_sb2 = [sbA(f"rdb_sb{i}", [2, TPC], BF16) for i in range(2)]
    b_rdb2 = [Buf(f"rdb_sb{i}") for i in range(2)]
    iota132 = sbA("iota132", [128, 132]); b_i132 = Buf("iota132")
    iota128f = sbA("iota128f", [128, 128]); b_i128 = Buf("iota128f")
    gidf_all = sbA("gidf_all", [128, NTC])
    gidf_sb = [gidf_all[:, i:i + 1] for i in range(NTC)]
    b_gidf = [Buf(f"gidf{i}") for i in range(NTC)]
    hi_i = sbA("hi_i", [128, 1], I32); b_hii = Buf("hi_i")
    lo_i = sbA("lo_i", [128, 1], I32); b_loi = Buf("lo_i")
    hi_f = sbA("hi_f", [128, 1]); b_hif = Buf("hi_f")
    lo_f = sbA("lo_f", [128, 1]); b_lof = Buf("lo_f")
    Afm = sbA("Afm", [128, 132]); b_Afm = Buf("Afm")
    Bfm = sbA("Bfm", [128, 128]); b_Bfm = Buf("Bfm")
    Bg = sbA("Bg", [128, 128]); b_Bg = Buf("Bg")
    tabsb_i = sbA("tabsb_i", [128, 132], I32); b_tbi = Buf("tabsb_i")
    tabsb_f = sbA("tabsb_f", [128, 132]); b_tbf = Buf("tabsb_f")

    esPA = ExitStack()
    kps2 = [esPA.enter_context(nc.psum_tensor(f"kps{i}", [128, C], F32)) for i in range(2)]
    b_kps2 = [Buf(f"kps{i}") for i in range(2)]
    vps = esPA.enter_context(nc.psum_tensor("vps", [128, 512], F32)); b_vps = Buf("vps")
    kvps = esPA.enter_context(nc.psum_tensor("kvps", [128, C], F32)); b_kvq = Buf("kvps")
    ksps = esPA.enter_context(nc.psum_tensor("ksps", [128, NIC], F32)); b_ksq = Buf("ksps")

    for i in range(NTC):
        dma("sync", gid_sb[i][:, :], gid_h[i * 128:(i + 1) * 128, :], writes=[b_gid[i]],
            sem="dL")
        dma("sync", xfr_sb[i][:, :], xfrow_h[i * 128:(i + 1) * 128, :], writes=[b_xfr[i]],
            sem="dL")
    for j in range(NIC):
        dma("gpsimd", XF[j][:, :], x_fm_h[j * 128:(j + 1) * 128, :], writes=[bXF[j]],
            sem="dL")
        dma("gpsimd", KT[j][:, :], wkT_h[j * 128:(j + 1) * 128, :], writes=[bKT[j]],
            sem="dL")
        dma("gpsimd", VT[j][:, :], wvT_h[j * 128:(j + 1) * 128, :], writes=[bVT[j]],
            sem="dL")
        dma("sync", wrf[j][:, :], wr_h[j * 128:(j + 1) * 128, :], writes=[b_wrf[j]],
            sem="dL")
        dma("gpsimd", wrbs[:, j * E:(j + 1) * E], wr_h[j * 128:(j + 1) * 128, :],
            writes=[b_wrb] if j == 0 else [], sem="dL")
    dma("sync", bqc[:, :], bq_cols_h[:, :], writes=[b_bqc], sem="dL")
    dma("sync", boc[:, :], bo_cols_h[:, :], writes=[b_boc], sem="dL")
    dma("gpsimd", bkb[:, :], bk_row_h[:, :], writes=[b_bkb], sem="dL")
    dma("gpsimd", bvb[:, :], bv_row_h[:, :], writes=[b_bvb], sem="dL")
    dma("gpsimd", bob[:, :], bo_row_h[:, :], writes=[b_bob], sem="dL")
    dma("sync", onesb[:, :], onesb_row_h[:, :], writes=[b_onesb], sem="dL")
    dma("sync", onesb_col[:, :], onesb_col_h[:, :], writes=[b_onesbc], sem="dL")
    dma("sync", ones_col[:, :], ones_col_h[:, :], writes=[b_onesc], sem="dL")
    dma("sync", ones_1x128[:, :], ones_1x128_h[:, :], writes=[b_o1], sem="dL")
    dma("sync", bdmask[:, :], bdmask_h[:, :], writes=[b_bdm], sem="dL")
    dma("sync", hmask2[:, :], hmask2_h[:, :], writes=[b_hm2], sem="dL")
    dma("sync", bmask[:, :], bmask_h[:, :], writes=[b_bm], sem="dL")
    dma("sync", ltri[:, :], ltri_h[:, :], writes=[b_ltr], sem="dL")
    dma("sync", iota8[:, :], iota8_h[:, :], writes=[b_io8], sem="dL")
    dma("sync", trash[:, :], trash_h[:, :], writes=[b_trh], sem="dL")
    dma("sync", ltme[:, :], ltme_h[:, :], writes=[b_ltm], sem="dL")
    for i in range(2):
        dma("sync", kvidx[i][:, :], kvidx_h[i * 128:(i + 1) * 128, :], writes=[b_kvx[i]],
            sem="dL")
    dma("sync", iota132[:, :], iota132_h[:, :], writes=[b_i132], sem="dL")
    dma("sync", iota128f[:, :], iota128f_h[:, :], writes=[b_i128], sem="dL")
    for i in range(NTC):
        dma("sync", gidf_sb[i], gidf_h[i * 128:(i + 1) * 128, :], writes=[b_gidf[i]],
            sem="dL")
        dma("sync", tbix[i][:, :], tabidx_h[i * 128:(i + 1) * 128, :], writes=[b_tbix[i]],
            sem="dL")
    tok_dL = ("dL", P.cnt["dL"])
    for b_ in (b_gid + b_xfr + bXF + bKT + bVT + b_wrf + b_gidf + b_tbix + b_kvx +
               [b_wrb, b_bqc, b_boc, b_bkb, b_bvb, b_bob, b_onesb, b_onesbc, b_onesc,
                b_o1, b_bdm, b_hm2, b_bm, b_ltr, b_io8, b_trh, b_ltm, b_i132, b_i128]):
        b_.w = tok_dL

    for j in range(NIC):
        dma("gpsimd", QT[j][:, :], wqT_h[j * 128:(j + 1) * 128, :], writes=[bQT[j]],
            sem="dQT")
        dma("gpsimd", OT[j][:, :], woT_h[j * 128:(j + 1) * 128, :], writes=[bOT[j]],
            sem="dOT")
    for j in range(NIC):
        bQT[j].w = ("dQT", P.cnt["dQT"])
        bOT[j].w = ("dOT", P.cnt["dOT"])

    def A_proj(tc):
        pi = tc % 2
        kps = kps2[pi]; b_kps = b_kps2[pi]
        xsl = [XF[ic][:, tc * 128:(tc + 1) * 128] for ic in range(NIC)]
        for jh in range(2):
            sl = slice(jh * 512, (jh + 1) * 512)
            mmgrp(kps[:, sl],
                  [(xsl[ic], KT[ic][:, sl]) for ic in range(NIC)] +
                  [(onesb[:, 0:128], bkb[:, sl])],
                  reads=bXF + bKT + [b_onesb, b_bkb], writes=[b_kps])
        for jh in range(2):
            sl = slice(jh * 512, (jh + 1) * 512)
            mmgrp(vps[:, :],
                  [(xsl[ic], VT[ic][:, sl]) for ic in range(NIC)] +
                  [(onesb[:, 0:128], bvb[:, sl])],
                  reads=bXF + bVT + [b_onesb, b_bvb], writes=[b_vps])
            tve = vop(lambda Eg, sl=sl, pi=pi: Eg.tensor_copy(vbt[pi][:, sl], vps[:, :]),
                      reads=[b_vps], writes=[b_vb[pi]] if jh == 0 else [],
                      extra=() if jh == 0 else (b_vb[pi].w,))
            if jh == 1:
                b_vb[pi].w = tve
        for jh in range(2):
            sl = slice(jh * 512, (jh + 1) * 512)
            scs = sc2sets[jh]; bscs = bsc2[jh]
            t1 = vop(lambda Eg, sl=sl, kps=kps, scs=scs: Eg.tensor_scalar(
                scs[0][:, :], kps[:, sl], 0.0, None, ALU.min),
                reads=[b_kps], writes=[bscs[0]])
            t2 = sop(lambda Eg, scs=scs: Eg.activation(out=scs[1][:, :], in_=scs[0][:, :],
                                                       func=AF.Exp),
                     reads=[bscs[0]], writes=[bscs[1]])
            t3 = sop(lambda Eg, sl=sl, kps=kps, scs=scs: Eg.activation(
                out=scs[2][:, :], in_=kps[:, sl], func=AF.Relu),
                reads=[b_kps], writes=[bscs[2]])
            tke = vop(lambda Eg, sl=sl, pi=pi, scs=scs: Eg.tensor_add(
                kbt[pi][:, sl], scs[2][:, :], scs[1][:, :]),
                reads=[bscs[1], bscs[2]], writes=[b_kb[pi]] if jh == 0 else [],
                extra=() if jh == 0 else (b_kb[pi].w,))
            if jh == 1:
                b_kb[pi].w = tke

    def A_acc(tc):
        pi = tc % 2
        for p in range(8):
            sl = slice(p * 128, (p + 1) * 128)
            tok = P.emit("tensor",
                         lambda Eg, o=kvps[:, sl], l=kbt[pi][:, sl], r=vbt[pi][:, sl],
                                s=(tc == 0 and p in (0, 4)), e=(tc == NTC - 1):
                         Eg.matmul(o, l, r, start=s, stop=e),
                         reads=[b_kb[pi], b_vb[pi]],
                         writes=[b_kvq] if (tc == 0 and p == 0) else [],
                         inc=("tP", 1), bookkeep=False)
            b_kb[pi].readers.append(tok); b_vb[pi].readers.append(tok)
            if tc == 0 and p == 0:
                b_kvq.w = tok
            if tc == NTC - 1 and p == 7:
                b_kvq.w = tok
        for jc in range(NIC):
            tok = P.emit("tensor",
                         lambda Eg, o=ksps[:, jc:jc + 1],
                                l=kbt[pi][:, jc * 128:(jc + 1) * 128],
                                r=onesb_col[:, :], s=(tc == 0 and jc == 0), e=(tc == NTC - 1):
                         Eg.matmul(o, l, r, start=s, stop=e),
                         reads=[b_kb[pi], b_onesbc],
                         writes=[b_ksq] if (tc == 0 and jc == 0) else [],
                         inc=("tP", 1), bookkeep=False)
            b_kb[pi].readers.append(tok)
            if (tc == 0 and jc == 0) or (tc == NTC - 1 and jc == NIC - 1):
                b_ksq.w = tok

    for tc in range(NTC):
        A_proj(tc)
        if tc > 0:
            A_acc(tc - 1)
    A_acc(NTC - 1)

    vop(lambda Eg: Eg.tensor_copy(kv_sb[:, 0:1024], kvps[:, :]), reads=[b_kvq], writes=[b_kvs])
    tks = vop(lambda Eg: Eg.tensor_copy(kv_sb[:, 1024:1032], ksps[:, :]),
              reads=[b_ksq], extra=(b_kvs.w,))
    b_kvs.w = tks
    dma("gpsimd", cc_kv_in[:, :], kv_sb[:, :], reads=[b_kvs], writes=[d_kvin], sem="dKV")
    coll("AllGather", ALU.bypass, cc_kv_in[:, :], cc_kv_out[:, :],
         reads=[d_kvin], writes=[d_kvout])

    P.barrier("tensor", P.all_tokens())
    esPA.close()
    esPB1 = ExitStack()
    qps2 = [esPB1.enter_context(nc.psum_tensor(f"qps{i}", [128, 512], F32)) for i in range(2)]
    b_qps2 = [Buf(f"qps{i}") for i in range(2)]
    rpps = esPB1.enter_context(nc.psum_tensor("rpps", [128, E], F32)); b_rpq = Buf("rpps")
    wwps = esPB1.enter_context(nc.psum_tensor("wwps", [128, E * NIC], F32)); b_wwq = Buf("wwps")

    mmgrp(rpps[0:1, :], [(ones_col[:, :], wrf[j][:, :]) for j in range(NIC)],
          reads=[b_onesc] + b_wrf, writes=[b_rpq])
    vop(lambda Eg: Eg.tensor_copy(wrsum_row[:, :], rpps[0:1, :]), reads=[b_rpq], writes=[b_wsr])
    mmgrp(rpps[0:1, :], [(boc[:, j:j + 1], wrf[j][:, :]) for j in range(NIC)],
          reads=[b_boc] + b_wrf, writes=[b_rpq])
    vop(lambda Eg: Eg.tensor_copy(bowr_row[:, :], rpps[0:1, :]), reads=[b_rpq], writes=[b_bwr])
    vop(lambda Eg: Eg.tensor_copy(bowr_bf[:, :], bowr_row[:, :]), reads=[b_bwr], writes=[b_bwrb])
    mmgrp(rpps[:, :], [(ones_1x128[:, :], wrsum_row[:, :])],
          reads=[b_o1, b_wsr], writes=[b_rpq])
    vop(lambda Eg: Eg.tensor_copy(wrsum_bc[:, :], rpps[:, :]), reads=[b_rpq], writes=[b_wsb])
    for jc in range(NIC):
        dma("sync", rt[:, 0:1024], wo_h[jc * 128:(jc + 1) * 128, :], writes=[b_rt])
        for cc in range(NIC):
            tok = P.emit("tensor",
                         lambda Eg, o=wwps[:, cc * E:(cc + 1) * E],
                                l=rt[:, cc * 128:(cc + 1) * 128], r=wrf[jc][:, :],
                                s=(jc == 0 and cc == 0), e=(jc == NIC - 1):
                         Eg.matmul(o, l, r, start=s, stop=e),
                         reads=[b_rt, b_wrf[jc]],
                         writes=[b_wwq] if (jc == 0 and cc == 0) else [],
                         inc=("tP", 1), bookkeep=False)
            b_rt.readers.append(tok); b_wrf[jc].readers.append(tok)
            if (jc == 0 and cc == 0) or (jc == NIC - 1 and cc == NIC - 1):
                b_wwq.w = tok
    for cc in range(NIC):
        vop(lambda Eg, cc=cc: Eg.tensor_copy(wowrb[cc][:, :], wwps[:, cc * E:(cc + 1) * E]),
            reads=[b_wwq], writes=[b_wwb[cc]])

    for jc in range(NIC):
        for tq in range(4):
            qi = (jc * 4 + tq) % 2
            sl = slice(tq * 512, (tq + 1) * 512)
            mmgrp(qps2[qi][:, :],
                  [(QT[ic][:, jc * 128:(jc + 1) * 128], XF[ic][:, sl]) for ic in range(NIC)],
                  reads=bQT + bXF, writes=[b_qps2[qi]])
            scs = sc2sets[qi]; bscs = bsc2[qi]
            t1 = vop(lambda Eg, jc=jc, qi=qi, scs=scs: Eg.tensor_scalar(
                scs[0][:, :], qps2[qi][:, :], bqc[:, jc:jc + 1], 0.0, ALU.add, ALU.min),
                reads=[b_qps2[qi], b_bqc], writes=[bscs[0]])
            t2 = sop(lambda Eg, scs=scs: Eg.activation(out=scs[1][:, :], in_=scs[0][:, :],
                                                       func=AF.Exp),
                     reads=[bscs[0]], writes=[bscs[1]])
            t3 = sop(lambda Eg, jc=jc, qi=qi, scs=scs: Eg.activation(
                out=scs[2][:, :], in_=qps2[qi][:, :], func=AF.Relu, bias=bqc[:, jc:jc + 1]),
                reads=[b_qps2[qi], b_bqc], writes=[bscs[2]])
            tq4 = vop(lambda Eg, jc=jc, sl=sl, scs=scs: Eg.tensor_add(
                QB[jc][:, sl], scs[2][:, :], scs[1][:, :]),
                reads=[bscs[1], bscs[2]], writes=[bQB[jc]] if tq == 0 else [],
                extra=() if tq == 0 else (bQB[jc].w,))
            if tq == 3:
                bQB[jc].w = tq4
    for tc in range(NTC):
        mmgrp(rpps[:, :],
              [(XF[ic][:, tc * 128:(tc + 1) * 128], wrbs[:, ic * E:(ic + 1) * E])
               for ic in range(NIC)],
              reads=bXF + [b_wrb], writes=[b_rpq])
        vop(lambda Eg, tc=tc: Eg.tensor_copy(RP[tc][:, :], rpps[:, :]),
            reads=[b_rpq], writes=[b_RP[tc]])

    idma(kv_sb[:, :], None, cc_kv_out[:, :], IndirectOffsetOnAxis(kvidx[0][:, :], 0),
         reads=[d_kvout, b_kvx[0]], writes=[b_kvs], sem="dKV")
    idma(rt[:, 0:1032], None, cc_kv_out[:, :], IndirectOffsetOnAxis(kvidx[1][:, :], 0),
         reads=[d_kvout, b_kvx[1]], writes=[b_rt], sem="dKV")
    vop(lambda Eg, rt=rt: Eg.tensor_add(kv_sb[:, :], rt[:, 0:1032], kv_sb[:, :]),
        reads=[b_rt, b_kvs], writes=[b_kvs])
    for p8_ in range(8):
        vop(lambda Eg, p8_=p8_: Eg.tensor_mul(
            kvb[:, p8_ * 128:(p8_ + 1) * 128], kv_sb[:, p8_ * 128:(p8_ + 1) * 128],
            bdmask[:, :]),
            reads=[b_kvs, b_bdm], writes=[b_kvb] if p8_ == 0 else [],
            extra=() if p8_ == 0 else (b_kvb.w,))
    b_kvb.w = ("tV", P.cnt["tV"])
    for p in range(8):
        vop(lambda Eg, p=p: Eg.tensor_scalar(
            ks2[:, 2 * p:2 * p + 2], hmask2[:, :], kv_sb[:, 1024 + p:1025 + p], None,
            ALU.mult),
            reads=[b_hm2, b_kvs], writes=[b_ks2] if p == 0 else [],
            extra=() if p == 0 else (b_ks2.w,))
    b_ks2.w = ("tV", P.cnt["tV"])

    P.barrier("tensor", P.all_tokens())
    esPB1.close()
    esPB2 = ExitStack()
    dnps2 = [esPB2.enter_context(nc.psum_tensor(f"dnps{i}", [2, 512], F32)) for i in range(2)]
    b_dnq2 = [Buf(f"dnps{i}") for i in range(2)]
    bcps = [esPB2.enter_context(nc.psum_tensor(f"bcps{i}", [128, 512], F32)) for i in range(2)]
    b_bcq = [Buf(f"bcps{i}") for i in range(2)]
    atps = [esPB2.enter_context(nc.psum_tensor(f"atps{i}", [128, 512], F32)) for i in range(2)]
    b_atq = [Buf(f"atps{i}") for i in range(2)]

    for p in range(8):
        pp = p % 2
        b_rd = b_rd2[pp]; b_rdb = b_rdb2[pp]
        rd_sb = rd_sb2[pp]; rdb_sb = rdb_sb2[pp]
        dnps = dnps2[pp]; b_dnq = b_dnq2[pp]
        for tq in range(4):
            sl = slice(tq * 512, (tq + 1) * 512)
            mmgrp(dnps[:, :], [(ks2[:, 2 * p:2 * p + 2], QB[p][:, sl])],
                  reads=[b_ks2, bQB[p]], writes=[b_dnq])
            tmax = vop(lambda Eg, sl=sl, rd_sb=rd_sb, dnps=dnps: Eg.tensor_scalar(
                rd_sb[:, sl], dnps[:, :], 1e-6, None, ALU.max),
                reads=[b_dnq], writes=[b_rd] if tq == 0 else [],
                extra=() if tq == 0 else (b_rd.w,))
            if tq == 3:
                b_rd.w = tmax
        tl_ = sop(lambda Eg, rd_sb=rd_sb: Eg.activation(out=rd_sb[:, :], in_=rd_sb[:, :],
                                                        func=AF.Ln),
                  reads=[b_rd], writes=[b_rd])
        tcb = sop(lambda Eg, rd_sb=rd_sb, rdb_sb=rdb_sb: Eg.activation(
            out=rdb_sb[:, :], in_=rd_sb[:, :], func=AF.Exp, scale=-1.0),
            reads=[b_rd], writes=[b_rdb])
        for tq in range(4):
            sl = slice(tq * 512, (tq + 1) * 512)
            pb = tq % 2
            mmgrp(bcps[pb][:, :], [(bmask[:, :], rdb_sb[:, sl])],
                  reads=[b_bm, b_rdb], writes=[b_bcq[pb]])
            tqq = vop(lambda Eg, p=p, sl=sl, pb=pb: Eg.tensor_mul(
                XF[p][:, sl], QB[p][:, sl], bcps[pb][:, :]),
                reads=[bQB[p], b_bcq[pb]], writes=[bXF[p]] if tq == 0 else [],
                extra=() if tq == 0 else (bXF[p].w,))
            if tq == 3:
                bXF[p].w = tqq
            mmgrp(atps[pb][:, :], [(kvb[:, p * 128:(p + 1) * 128], XF[p][:, sl])],
                  reads=[b_kvb, bXF[p]], writes=[b_atq[pb]], extra=(tqq,))
            tev = vop(lambda Eg, p=p, sl=sl, pb=pb: Eg.tensor_copy(
                QB[p][:, sl], atps[pb][:, :]),
                reads=[b_atq[pb]], writes=[])
            bQB[p].readers.append(tev)
            if tq == 0:
                bATB_w0 = tev
            if tq == 3:
                bQB[p].w = tev
                bQB[p].readers = []

    ATB = QB
    bATB = bQB

    P.barrier("tensor", P.all_tokens())
    esPB2.close()
    esPB3 = ExitStack()
    aops2 = [esPB3.enter_context(nc.psum_tensor(f"aops{i}", [128, C], F32)) for i in range(2)]
    b_aoq2 = [Buf(f"aops{i}") for i in range(2)]
    lgps2 = [esPB3.enter_context(nc.psum_tensor(f"lgps{i}", [128, E], F32)) for i in range(2)]
    b_lgq2 = [Buf(f"lgps{i}") for i in range(2)]
    cnps = esPB3.enter_context(nc.psum_tensor("cnps", [1, E], F32)); b_cnq = Buf("cnps")

    tz = vop(lambda Eg: Eg.memset(crow[0][:, :], 0.0), writes=[b_crw[0]])
    dma("sync", Xs[0][:, :], x_tm_h[0:128, :], writes=[bXs[0]], sem="dX")
    dma("sync", Xs[1][:, :], x_tm_h[128:256, :], writes=[bXs[1]], sem="dX")
    for tc in range(NTC):
        ai = tc % 2
        aops = aops2[ai]; b_aoq = b_aoq2[ai]
        lgps = lgps2[ai]; b_lgq = b_lgq2[ai]
        rt = rt_2[ai]; b_rt = b_rt_2[ai]
        xfb = xfb_2[ai]; b_xfb = b_xfb_2[ai]
        ln_c = lnc_2[ai]; b_lnc = b_lnc_2[ai]
        top8 = t8_2[ai]; b_t8 = b_t8_2[ai]
        tmp8a = m8a_2[ai]; b_m8a = b_m8a_2[ai]
        tmp8b = m8b_2[ai]; b_m8b = b_m8b_2[ai]
        tmp1b = m1b_2[ai]; b_m1b = b_m1b_2[ai]
        for jh in range(2):
            sl = slice(jh * 512, (jh + 1) * 512)
            mmgrp(aops[:, sl],
                  [(ATB[cc][:, tc * 128:(tc + 1) * 128], OT[cc][:, sl]) for cc in range(NIC)] +
                  [(onesb[:, 0:128], bob[:, sl])],
                  reads=bATB + bOT + [b_onesb, b_bob], writes=[b_aoq])
        mmgrp(lgps[:, :],
              [(ATB[cc][:, tc * 128:(tc + 1) * 128], wowrb[cc][:, :]) for cc in range(NIC)] +
              [(onesb[:, 0:128], bowr_bf[:, :])],
              reads=bATB + b_wwb + [b_onesb, b_bwrb], writes=[b_lgq])
        t_r = vop(lambda Eg, tc=tc, aops=aops, rt=rt, ln_c=ln_c: Eg.scalar_tensor_tensor(
            rt[:, 0:1024], aops[:, :], 1.0, Xs[tc % 2][:, :], ALU.mult, ALU.add,
            accum_out=ln_c[:, 0:1]),
            reads=[b_aoq, bXs[tc % 2]], writes=[b_rt, b_lnc])
        if tc + 2 < NTC:
            dma("sync", Xs[tc % 2][:, :], x_tm_h[(tc + 2) * 128:(tc + 3) * 128, :],
                writes=[bXs[tc % 2]], sem="dX")
        t_v = sop(lambda Eg, rt=rt, xfb=xfb, ln_c=ln_c: Eg.activation(
            out=xfb[:, :], in_=rt[:, 0:1024], func=AF.Square, accum_out=ln_c[:, 2:3]),
            reads=[b_rt], writes=[b_xfb, b_lnc])
        t_m = vop(lambda Eg, ln_c=ln_c: Eg.tensor_scalar(ln_c[:, 1:2], ln_c[:, 0:1],
                                                         1.0 / C, None, ALU.mult),
                  reads=[b_lnc], writes=[b_lnc])
        t_e = vop(lambda Eg, ln_c=ln_c: Eg.tensor_scalar(ln_c[:, 3:4], ln_c[:, 2:3],
                                                         1.0 / C, 1e-5, ALU.mult, ALU.add),
                  reads=[b_lnc], writes=[b_lnc])
        t_s1 = vop(lambda Eg, ln_c=ln_c: Eg.tensor_scalar(
            ln_c[:, 4:5], ln_c[:, 1:2], ln_c[:, 1:2], None, ALU.mult),
            reads=[b_lnc], writes=[b_lnc])
        t_s2 = vop(lambda Eg, ln_c=ln_c: Eg.tensor_sub(
            ln_c[:, 5:6], ln_c[:, 3:4], ln_c[:, 4:5]),
            reads=[b_lnc], writes=[b_lnc])
        t_q = sop(lambda Eg, ln_c=ln_c: Eg.activation(out=ln_c[:, 5:6], in_=ln_c[:, 5:6],
                                                      func=AF.Ln),
                  reads=[b_lnc], writes=[b_lnc])
        t_i = sop(lambda Eg, ln_c=ln_c: Eg.activation(out=ln_c[:, 5:6], in_=ln_c[:, 5:6],
                                                      func=AF.Exp, scale=-0.5),
                  reads=[b_lnc], writes=[b_lnc])
        t_s3 = vop(lambda Eg, ln_c=ln_c: Eg.tensor_scalar(
            ln_c[:, 6:7], ln_c[:, 1:2], ln_c[:, 5:6], None, ALU.mult),
            reads=[b_lnc], writes=[b_lnc])
        t_x = vop(lambda Eg, rt=rt, xfb=xfb, ln_c=ln_c: Eg.tensor_scalar(
            xfb[:, :], rt[:, 0:1024], ln_c[:, 5:6], ln_c[:, 6:7], ALU.mult, ALU.subtract),
            reads=[b_rt, b_lnc], writes=[b_xfb])
        dma("sync", cc_xf_in[tc * 128:(tc + 1) * 128, :], xfb[:, :],
            reads=[b_xfb], writes=[d_xfin] if tc == 0 else [],
            extra=() if tc == 0 else (d_xfin.w,), sem="dF")
        d_xfin.w = ("dF", P.cnt["dF"])
        if tc % 4 == 3 and tc < NTC - 1:
            q_ = tc // 4
            coll("AllGather", ALU.bypass,
                 cc_xf_in[q_ * 512:(q_ + 1) * 512, :],
                 cc_xf_out[q_ * 4096:(q_ + 1) * 4096, :],
                 reads=[d_xfin], writes=[d_xfout] if q_ == 0 else [],
                 extra=() if q_ == 0 else (d_xfout.w,))
        t_l1 = vop(lambda Eg, tc=tc, tmp8a=tmp8a, ln_c=ln_c: Eg.tensor_scalar(
            tmp8a[:, :], wrsum_bc[:, :], ln_c[:, 1:2], None, ALU.mult),
            reads=[b_wsb, b_lnc], writes=[b_m8a])
        t_l2 = vop(lambda Eg, tc=tc, lgps=lgps, tmp8a=tmp8a, tmp8b=tmp8b: Eg.tensor_sub(
            tmp8b[:, :], lgps[:, :], tmp8a[:, :]),
            reads=[b_lgq, b_m8a], writes=[b_m8b])
        t_l3 = vop(lambda Eg, tc=tc, tmp8a=tmp8a, tmp8b=tmp8b: Eg.tensor_add(
            tmp8a[:, :], tmp8b[:, :], RP[tc][:, :]),
            reads=[b_m8b, b_RP[tc]], writes=[b_m8a])
        t_l4 = vop(lambda Eg, tc=tc, tmp8a=tmp8a, ln_c=ln_c: Eg.tensor_scalar(
            lgs[tc][:, :], tmp8a[:, :], ln_c[:, 5:6], None, ALU.mult),
            reads=[b_m8a, b_lnc], writes=[b_lgs[tc]])
        t_rm = vop(lambda Eg, tc=tc, top8=top8: Eg.reduce_max(
            top8[:, 0:1], lgs[tc][:, :], axis=AX.X),
            reads=[b_lgs[tc]], writes=[b_t8])
        t_nm = vop(lambda Eg, tc=tc, top8=top8: Eg.tensor_scalar(
            top8[:, 1:2], top8[:, 0:1], -1.0, None, ALU.mult),
            reads=[b_t8], writes=[b_t8])
        t_oh = vop(lambda Eg, tc=tc, top8=top8: Eg.tensor_scalar(
            onehot[tc][:, :], lgs[tc][:, :], top8[:, 0:1], None, ALU.is_equal),
            reads=[b_lgs[tc], b_t8], writes=[b_oh[tc]])
        t_mu = vop(lambda Eg, tc=tc, tmp8b=tmp8b: Eg.tensor_mul(
            tmp8b[:, :], onehot[tc][:, :], iota8[:, :]),
            reads=[b_oh[tc], b_io8], writes=[b_m8b])
        tokc = P.emit("tensor",
                      lambda Eg, o=cnps[:, :], l=ones_col[:, :], r=onehot[tc][:, :],
                             s=(tc == 0), e=(tc == NTC - 1):
                      Eg.matmul(o, l, r, start=s, stop=e),
                      reads=[b_onesc, b_oh[tc]],
                      writes=[b_cnq] if tc == 0 else [],
                      inc=("tP", 1), bookkeep=False)
        b_oh[tc].readers.append(tokc)
        if tc == 0 or tc == NTC - 1:
            b_cnq.w = tokc
        t_ei = vop(lambda Eg, tc=tc, tmp8b=tmp8b: Eg.reduce_sum(
            eidx_f[tc][:, :], tmp8b[:, :], axis=AX.X),
            reads=[b_m8b], writes=[b_ei[tc]])
        t_ex = sop(lambda Eg, tc=tc, tmp8b=tmp8b, top8=top8, tmp1b=tmp1b: Eg.activation(
            out=tmp8b[:, :], in_=lgs[tc][:, :], func=AF.Exp, bias=top8[:, 1:2],
            accum_out=tmp1b[:, 0:1]),
            reads=[b_lgs[tc], b_t8], writes=[b_m8b, b_m1b])
        t_pt = vop(lambda Eg, tc=tc, tmp1b=tmp1b: Eg.reciprocal(ptok[tc][:, :], tmp1b[:, :]),
                   reads=[b_m1b], writes=[b_ptk[tc]])
        if debug_stage == 2:
            P.emit("sync", lambda Eg, tc=tc: Eg.dma_start(
                out=dbg["mr"][tc * 128:(tc + 1) * 128, 0:1], in_=eidx_f[tc][:, :],
                allow_slow_non_contiguous=True),
                reads=[b_ei[tc]], inc=("dS", 16))
            P.emit("sync", lambda Eg, tc=tc: Eg.dma_start(
                out=dbg["mr"][tc * 128:(tc + 1) * 128, 1:2], in_=ptok[tc][:, :],
                allow_slow_non_contiguous=True),
                reads=[b_ptk[tc]], inc=("dS", 16))

    if debug_stage == 2:
        for cc2 in range(NIC):
            dma("sync", dbg["wowr"][:, cc2 * E:(cc2 + 1) * E], wowrb[cc2][:, :],
                reads=[b_wwb[cc2]])
    if debug_stage == 2:
        for tcd in range(NTC):
            P.emit("sync", lambda Eg, tcd=tcd: Eg.dma_start(
                out=dbg["mr"][tcd * 128:(tcd + 1) * 128, 2:3], in_=eidx_f[tcd][:, :],
                allow_slow_non_contiguous=True),
                reads=[b_ei[tcd]], inc=("dS", 16))
    vop(lambda Eg: Eg.tensor_copy(cnt_row[:, :], cnps[:, :]), reads=[b_cnq], writes=[b_cnr])
    dma("gpsimd", cc_cnt_in[:, :], cnt_row[:, :], reads=[b_cnr], writes=[d_cntin], sem="dCNT")
    coll("AllGather", ALU.bypass, cc_cnt_in[:, :], cc_cnt_out[:, :],
         reads=[d_cntin], writes=[d_cntout])
    coll("AllGather", ALU.bypass,
         cc_xf_in[3 * 512:4 * 512, :],
         cc_xf_out[3 * 4096:4 * 4096, :],
         reads=[d_xfin], extra=(d_xfout.w,))
    d_xfout.w = ("cc", P.cnt["cc"])

    P.barrier("tensor", P.all_tokens())
    esPB3.close()
    esPR = ExitStack()
    rkps = esPR.enter_context(nc.psum_tensor("rkps", [128, E], F32)); b_rkq = Buf("rkps")
    csps = esPR.enter_context(nc.psum_tensor("csps", [1, E], F32)); b_csq = Buf("csps")
    for tc in range(NTC):
        pr = tc % 2
        mmgrp(rkps[:, :],
              [(ltri[:, :], onehot[tc][:, :]), (ones_1x128[:, :], crow[pr][:, :])],
              reads=[b_ltr, b_oh[tc], b_o1, b_crw[pr]], writes=[b_rkq])
        vop(lambda Eg, tc=tc: Eg.tensor_copy(rank_sb[tc][:, :], rkps[:, :]),
            reads=[b_rkq], writes=[b_rk[tc]])
        mmgrp(csps[:, :], [(ones_col[:, :], onehot[tc][:, :])],
              reads=[b_onesc, b_oh[tc]], writes=[b_csq])
        vop(lambda Eg: Eg.tensor_copy(ccol[:, :], csps[:, :]), reads=[b_csq], writes=[b_ccl])
        vop(lambda Eg, pr=pr: Eg.tensor_add(crow[1 - pr][:, :], crow[pr][:, :], ccol[:, :]),
            reads=[b_crw[pr], b_ccl], writes=[b_crw[1 - pr]])

    dma("gpsimd", cnt8[:, :], cc_cnt_out[:, :], reads=[d_cntout], writes=[b_cn8], sem="dCNT")
    P.barrier("tensor", P.all_tokens())
    esPR.close()
    esPC = ExitStack()
    bsps = esPC.enter_context(nc.psum_tensor("bsps", [128, E], F32)); b_bsq = Buf("bsps")
    mmgrp(bsps[0:1, :], [(ltme[:, :], cnt8[:, :])], reads=[b_ltm, b_cn8], writes=[b_bsq])
    vop(lambda Eg: Eg.tensor_copy(base_row[:, :], bsps[0:1, :]), reads=[b_bsq], writes=[b_bsr])
    mmgrp(bsps[:, :], [(ones_1x128[:, :], base_row[:, :])],
          reads=[b_o1, b_bsr], writes=[b_bsq])
    vop(lambda Eg: Eg.tensor_copy(base_bc[:, :], bsps[:, :]), reads=[b_bsq], writes=[b_bbc])

    tabps = esPC.enter_context(nc.psum_tensor("tabps", [128, 132], F32))
    b_tbq = Buf("tabps")
    vop(lambda Eg: Eg.tensor_add(
        cm8a[:, :], rank_all[:, :],
        base_bc[:, :].rearrange("p e -> p () e").broadcast_to([128, NTC, E])),
        reads=b_rk + [b_bbc], writes=[b_cm8a])
    vop(lambda Eg: Eg.tensor_mul(cm8b[:, :], cm8a[:, :], oh_all[:, :]),
        reads=[b_cm8a] + b_oh, writes=[b_cm8b])
    for tc in range(NTC):
        vop(lambda Eg, tc=tc: Eg.reduce_sum(
            gsl_all[:, tc:tc + 1], cm8b[:, tc * E:(tc + 1) * E], axis=AX.X),
            reads=[b_cm8b], writes=[b_gsla])
    vop(lambda Eg: Eg.tensor_scalar(kee_all[:, :], gsl_all[:, :], float(CAP), None,
                                    ALU.is_lt),
        reads=[b_gsla], writes=[b_keea])
    vop(lambda Eg: Eg.tensor_mul(psc_all[:, :], ptk_all[:, :], kee_all[:, :]),
        reads=b_ptk + [b_keea], writes=[b_psca])
    vop(lambda Eg: Eg.scalar_tensor_tensor(
        cm1a[:, :], ei_all[:, :], float(CAP), gsl_all[:, :], ALU.mult, ALU.add),
        reads=b_ei + [b_gsla], writes=[b_cm1a])
    vop(lambda Eg: Eg.tensor_scalar(cm1b[:, :], cm1a[:, :], trash[:, 0:1], None,
                                    ALU.subtract),
        reads=[b_cm1a, b_trh], writes=[b_cm1b])
    vop(lambda Eg: Eg.tensor_mul(cm1b[:, :], cm1b[:, :], kee_all[:, :]),
        reads=[b_cm1b, b_keea], writes=[b_cm1b])
    vop(lambda Eg: Eg.tensor_scalar(cm1c[:, :], cm1b[:, :], trash[:, 0:1], None,
                                    ALU.add),
        reads=[b_cm1b, b_trh], writes=[b_cm1c])
    vop(lambda Eg: Eg.tensor_copy(sx_all[:, :], cm1c[:, :]), reads=[b_cm1c], writes=[b_sxa])
    vop(lambda Eg: Eg.tensor_scalar(cm1a[:, :], gsl_all[:, :], 512.0, None, ALU.is_ge),
        reads=[b_gsla], writes=[b_cm1a])
    vop(lambda Eg: Eg.tensor_scalar(cm1b[:, :], gsl_all[:, :], 1024.0, None, ALU.is_ge),
        reads=[b_gsla], writes=[b_cm1b])
    vop(lambda Eg: Eg.tensor_add(cm1a[:, :], cm1a[:, :], cm1b[:, :]),
        reads=[b_cm1b, b_cm1a], writes=[b_cm1a])
    vop(lambda Eg: Eg.tensor_scalar(cm1b[:, :], gsl_all[:, :], 1536.0, None, ALU.is_ge),
        reads=[b_gsla], writes=[b_cm1b])
    vop(lambda Eg: Eg.tensor_add(cm1a[:, :], cm1a[:, :], cm1b[:, :]),
        reads=[b_cm1b, b_cm1a], writes=[b_cm1a])
    vop(lambda Eg: Eg.scalar_tensor_tensor(
        cm1b[:, :], cm1a[:, :], -512.0, gsl_all[:, :], ALU.mult, ALU.add),
        reads=[b_cm1a, b_gsla], writes=[b_cm1b])
    vop(lambda Eg: Eg.scalar_tensor_tensor(
        cm1c[:, :], cm1a[:, :], 4096.0, cm1b[:, :], ALU.mult, ALU.add),
        reads=[b_cm1a, b_cm1b], writes=[b_cm1c])
    vop(lambda Eg: Eg.scalar_tensor_tensor(
        cm1a[:, :], ei_all[:, :], 512.0, cm1c[:, :], ALU.mult, ALU.add),
        reads=b_ei + [b_cm1c], writes=[b_cm1a])
    vop(lambda Eg: Eg.tensor_mul(yrw_f[:, :], cm1a[:, :], kee_all[:, :]),
        reads=[b_cm1a, b_keea], writes=[b_yrwf])
    for tc in range(NTC):
        vop(lambda Eg, tc=tc: Eg.tensor_copy(yrow[tc][:, :], yrw_f[:, tc:tc + 1]),
            reads=[b_yrwf], writes=[b_yrw[tc]])
    vop(lambda Eg: Eg.tensor_scalar(hi_i_all[:, :], sx_all[:, :], 7, None,
                                    ALU.logical_shift_right),
        reads=[b_sxa], writes=[b_hiia])
    vop(lambda Eg: Eg.tensor_copy(hi_f_all[:, :], hi_i_all[:, :]),
        reads=[b_hiia], writes=[b_hifa])
    vop(lambda Eg: Eg.tensor_scalar(lo_i_all[:, :], sx_all[:, :], 127, None,
                                    ALU.bitwise_and),
        reads=[b_sxa], writes=[b_loia])
    vop(lambda Eg: Eg.tensor_copy(lo_f_all[:, :], lo_i_all[:, :]),
        reads=[b_loia], writes=[b_lofa])
    for tc in range(NTC):
        tA = vop(lambda Eg, tc=tc: Eg.tensor_scalar(
            Afm[:, :], iota132[:, :], hi_f_all[:, tc:tc + 1], None, ALU.is_equal),
            reads=[b_i132, b_hifa], writes=[b_Afm])
        tB = vop(lambda Eg, tc=tc: Eg.tensor_scalar(
            Bfm[:, :], iota128f[:, :], lo_f_all[:, tc:tc + 1], None, ALU.is_equal),
            reads=[b_i128, b_lofa], writes=[b_Bfm])
        tBg = vop(lambda Eg, tc=tc: Eg.tensor_scalar(Bg[:, :], Bfm[:, :],
                                                     gidf_sb[tc], None, ALU.mult),
                  reads=[b_Bfm, b_gidf[tc]], writes=[b_Bg])
        tok = P.emit("tensor",
                     lambda Eg, o=tabps[:, :], l=Bg[:, :], r=Afm[:, :],
                            s=(tc == 0), e=(tc == NTC - 1):
                     Eg.matmul(o, l, r, start=s, stop=e),
                     reads=[b_Bg, b_Afm],
                     writes=[b_tbq] if tc == 0 else [],
                     inc=("tP", 1), bookkeep=False)
        b_Bg.readers.append(tok); b_Afm.readers.append(tok)
        if tc == 0 or tc == NTC - 1:
            b_tbq.w = tok
    if debug_stage == 2:
        for tcd in range(NTC):
            P.emit("sync", lambda Eg, tcd=tcd: Eg.dma_start(
                out=dbg["mr"][tcd * 128:(tcd + 1) * 128, 3:4], in_=eidx_f[tcd][:, :],
                allow_slow_non_contiguous=True),
                reads=[b_ei[tcd]], inc=("dS", 16))
    vop(lambda Eg: Eg.tensor_copy(tabsb_f[:, :], tabps[:, :]), reads=[b_tbq], writes=[b_tbf])
    vop(lambda Eg: Eg.tensor_copy(tabsb_i[:, :], tabsb_f[:, :]), reads=[b_tbf], writes=[b_tbi])
    dma("gpsimd", table_2d, tabsb_i[:, :], reads=[b_tbi], writes=[d_tab], sem="dTAB")
    coll("AllReduce", ALU.add, table_2d, table_2d, reads=[d_tab], writes=[d_tab])

    if debug_stage == 1:
        dma("gpsimd", dbg["xln"], cc_xf_in[:, :], reads=[d_xfin], writes=[d_dbg])
    if debug_stage == 2:
        dma("gpsimd", dbg["table"], table_2d, reads=[d_tab], writes=[d_dbg],
            extra=[d_tab.w])
        dma("gpsimd", dbg["cnt"], cc_cnt_out[:, :], reads=[d_cntout], writes=[d_dbg])
        metasb = sb("metasb", [128, 4]); b_met = Buf("metasb")
        for tc in range(NTC):
            vop(lambda Eg, tc=tc: Eg.tensor_copy(metasb[:, 0:1], eidx_f[tc][:, :]),
                reads=[b_ei[tc]], writes=[b_met])
            t1 = vop(lambda Eg, tc=tc: Eg.tensor_copy(metasb[:, 1:2], gslot[tc][:, :]),
                     reads=[b_gsl[tc]], extra=(b_met.w,))
            t2 = vop(lambda Eg, tc=tc: Eg.tensor_copy(metasb[:, 2:3], keepc[tc][:, :]),
                     reads=[b_kee[tc]], extra=(t1,))
            t3 = vop(lambda Eg, tc=tc: Eg.tensor_copy(metasb[:, 3:4], ptok[tc][:, :]),
                     reads=[b_ptk[tc]], extra=(t2,))
            b_met.w = t3
            dma("sync", dbg["meta"][tc * 128:(tc + 1) * 128, :], metasb[:, :],
                reads=[b_met], sem="dDBG")
            dma("sync", dbg["lgs"][tc * 128:(tc + 1) * 128, :], lgs[tc][:, :],
                reads=[b_lgs[tc]])
            dma("sync", dbg["rp"][tc * 128:(tc + 1) * 128, :], RP[tc][:, :],
                reads=[b_RP[tc]])

    if debug_stage in (1, 2):
        P.full_barrier(exclude=())
        return nc, P

    P.full_barrier(exclude=("cc",))
    esPC.close()
    esA.close()

    esD = ExitStack()
    def sbD(name, shape, dt=F32):
        return esD.enter_context(nc.sbuf_tensor("sd_" + name, shape, dt))

    W1b = [sbD(f"W1b{dc}", [128, FF], BF16) for dc in range(NIC)]
    bW1 = [Buf(f"W1b{dc}") for dc in range(NIC)]
    W2b = [sbD(f"W2b{fc}", [128, C], BF16) for fc in range(NFC)]
    bW2 = [Buf(f"W2b{fc}") for fc in range(NFC)]
    hbt = [sbD(f"hb{fc}", [128, 512], BF16) for fc in range(NFC)]
    bhb = [Buf(f"hb{fc}") for fc in range(NFC)]
    xef = [[sbD(f"xef{i}_{dc}", [128, 512], BF16) for dc in range(NIC)] for i in range(2)]
    bxe = [[Buf(f"xef{i}_{dc}") for dc in range(NIC)] for i in range(2)]
    yebt = [sbD(f"yeb{i}", [128, C], BF16) for i in range(3)]
    bye = [Buf(f"yeb{i}") for i in range(3)]
    xg = [sbD(f"xg{i}", [128, C], BF16) for i in range(2)]
    bxg = [Buf(f"xg{i}") for i in range(2)]

    esPD = ExitStack()
    hps = [esPD.enter_context(nc.psum_tensor(f"hps{i}", [128, 512], F32)) for i in range(3)]
    bhq = [Buf(f"hps{i}") for i in range(3)]
    yps = [esPD.enter_context(nc.psum_tensor(f"yps{i}", [128, 512], F32)) for i in range(3)]
    byq = [Buf(f"yps{i}") for i in range(3)]

    for dc in range(NIC):
        dma("gpsimd", W1b[dc][:, :], w1e_h[dc * 128:(dc + 1) * 128, :], writes=[bW1[dc]],
            sem="dW1")
    for fc in range(NFC):
        dma("gpsimd", W2b[fc][:, :], w2e_h[fc * 128:(fc + 1) * 128, :], writes=[bW2[fc]],
            sem="dW2")

    xidx = [sbD(f"xidx{i}", [128, 1], I32) for i in range(NTC)]
    b_xi = [Buf(f"xidx{i}") for i in range(NTC)]
    xw1 = sbD("xw1", [128, NTC], I32); b_xw1 = Buf("xw1")
    xw2 = sbD("xw2", [128, NTC], I32); b_xw2 = Buf("xw2")
    xw3 = sbD("xw3", [128, NTC], I32); b_xw3 = Buf("xw3")
    for scn in range(NTC):
        idma(tabsb[scn][:, :], None, table_d[:, :], IndirectOffsetOnAxis(tbix[scn][:, :], 0),
             reads=[d_tab, b_tbix[scn]], writes=[b_tab[scn]], sem="dTAB")
    tok_tab_all = ("dTAB", P.cnt["dTAB"])
    for scn in range(NTC):
        b_tab[scn].w = tok_tab_all
    for scn in range(NTC):
        tcp = vop(lambda Eg, scn=scn: Eg.tensor_copy(xw1[:, scn:scn + 1], tabsb[scn][:, :]),
                  reads=[b_tab[scn]], writes=[b_xw1] if scn == 0 else [],
                  extra=() if scn == 0 else (b_xw1.w,))
        if scn == NTC - 1:
            b_xw1.w = tcp
    vop(lambda Eg: Eg.tensor_scalar(xw2[:, :], xw1[:, :], 2047, None, ALU.bitwise_and),
        reads=[b_xw1], writes=[b_xw2])
    vop(lambda Eg: Eg.tensor_scalar(xw2[:, :], xw2[:, :], 9, None, ALU.logical_shift_right),
        reads=[b_xw2], writes=[b_xw2])
    vop(lambda Eg: Eg.tensor_scalar(xw2[:, :], xw2[:, :], 12, None, ALU.logical_shift_left),
        reads=[b_xw2], writes=[b_xw2])
    vop(lambda Eg: Eg.tensor_scalar(xw3[:, :], xw1[:, :], 11, None, ALU.logical_shift_right),
        reads=[b_xw1], writes=[b_xw3])
    vop(lambda Eg: Eg.tensor_scalar(xw3[:, :], xw3[:, :], 9, None, ALU.logical_shift_left),
        reads=[b_xw3], writes=[b_xw3])
    vop(lambda Eg: Eg.tensor_add(xw2[:, :], xw2[:, :], xw3[:, :]),
        reads=[b_xw3, b_xw2], writes=[b_xw2])
    vop(lambda Eg: Eg.tensor_scalar(xw3[:, :], xw1[:, :], 511, None, ALU.bitwise_and),
        reads=[b_xw1], writes=[b_xw3])
    vop(lambda Eg: Eg.tensor_add(xw2[:, :], xw2[:, :], xw3[:, :]),
        reads=[b_xw3, b_xw2], writes=[b_xw2])
    for scn in range(NTC):
        vop(lambda Eg, scn=scn: Eg.tensor_copy(xidx[scn][:, :], xw2[:, scn:scn + 1]),
            reads=[b_xw2], writes=[b_xi[scn]])

    tok_store = []
    for scn in range(NTC):
        pb = scn % 2
        idma(xg[pb][:, :], None, cc_xf_out[:, :], IndirectOffsetOnAxis(xidx[scn][:, :], 0),
             reads=[d_xfout, b_xi[scn]], writes=[bxg[pb]], sem="dXB")
        ts_ = dma("gpsimd", xe_rows_d[scn * 128:(scn + 1) * 128, :], xg[pb][:, :],
                  reads=[bxg[pb]], writes=[d_xer] if scn == 0 else [],
                  extra=() if scn == 0 else (d_xer.w,), sem="dXB")
        tok_store.append(ts_)
        if scn == NTC - 1:
            d_xer.w = ("dXB", P.cnt["dXB"])

    for sq in range(4):
        pb = sq % 2
        for dc in range(NIC):
            dma_t(xef[pb][dc][:, :],
                  xe_rows_d[sq * 512:(sq + 1) * 512, dc * 128:(dc + 1) * 128],
                  reads=[], writes=[bxe[pb][dc]],
                  extra=[tok_store[4 * sq + j] for j in range(4)])
        for fc in range(NFC):
            hp = fc % 3
            dcg = fc // 4
            mmgrp(hps[hp][:, :],
                  [(W1b[dc][:, fc * 128:(fc + 1) * 128], xef[pb][dc][:, :])
                   for dc in range(NIC)],
                  reads=bW1 + bxe[pb], writes=[bhq[hp]])
            sop(lambda Eg, fc=fc, hp=hp: Eg.activation(
                out=hbt[fc][:, :], in_=hps[hp][:, :], func=AF.Relu),
                reads=[bhq[hp]], writes=[bhb[fc]])
        for sci in range(4):
            scn = sq * 4 + sci
            yb = sci % 3
            for jh in range(2):
                sl = slice(jh * 512, (jh + 1) * 512)
                mmgrp(yps[yb][:, :],
                      [(hbt[fc][:, sci * 128:(sci + 1) * 128], W2b[fc][:, sl])
                       for fc in range(NFC)],
                      reads=bhb + bW2, writes=[byq[yb]])
                vop(lambda Eg, yb=yb, sl=sl: Eg.tensor_copy(yebt[yb][:, sl], yps[yb][:, :]),
                    reads=[byq[yb]], writes=[bye[yb]] if jh == 0 else [],
                    extra=() if jh == 0 else (bye[yb].w,))
            bye[yb].w = ("tV", P.cnt["tV"])
            dma("sync", cc_ye_in[scn * 128:(scn + 1) * 128, :], yebt[yb][:, :],
                reads=[bye[yb]], writes=[d_yein] if scn == 0 else [],
                extra=() if scn == 0 else (d_yein.w,), sem="dYE")
            d_yein.w = ("dYE", P.cnt["dYE"])
        coll("AllGather", ALU.bypass,
             cc_ye_in[sq * 512:(sq + 1) * 512, :],
             cc_ye_out[sq * 4096:(sq + 1) * 4096, :],
             reads=[d_yein], writes=[d_yeout] if sq == 0 else [],
             extra=() if sq == 0 else (d_yeout.w,))
        if sq == 3:
            d_yeout.w = ("cc", P.cnt["cc"])

    if debug_stage == 3:
        dma("gpsimd", dbg["ye"], cc_ye_in[:, :], reads=[d_yein], writes=[d_dbg])
        P.full_barrier(exclude=())
        return nc, P

    P.full_barrier(exclude=("cc",))
    esPD.close()
    esD.close()

    esE = ExitStack()
    def sbE(name, shape, dt=F32):
        return esE.enter_context(nc.sbuf_tensor("se_" + name, shape, dt))
    ygt = [sbE(f"ygt{i}", [128, C], BF16) for i in range(2)]
    byg = [Buf(f"ygt{i}") for i in range(2)]
    xft = [sbE(f"xft{i}", [128, C], BF16) for i in range(2)]
    bxf2 = [Buf(f"xft{i}") for i in range(2)]
    rtE2 = [sbE(f"rtE{i}", [128, C]) for i in range(2)]
    b_rtE2 = [Buf(f"rtE{i}") for i in range(2)]
    sqE2 = [sbE(f"sqE{i}", [128, C]) for i in range(2)]
    b_sqE2 = [Buf(f"sqE{i}") for i in range(2)]
    lnE2 = [sbE(f"lnE{i}", [128, 8]) for i in range(2)]
    b_lnE2 = [Buf(f"lnE{i}") for i in range(2)]
    outE = [sbE(f"outE{i}", [128, C]) for i in range(2)]
    bout = [Buf(f"outE{i}") for i in range(2)]

    for tc in range(NTC):
        pb = tc % 2
        rtE = rtE2[pb]; b_rtE = b_rtE2[pb]
        sqE = sqE2[pb]; b_sqE = b_sqE2[pb]
        lnE = lnE2[pb]; b_lnE = b_lnE2[pb]
        idma(ygt[pb][:, :], None, cc_ye_out[:, :], IndirectOffsetOnAxis(yrow[tc][:, :], 0),
             reads=[d_yeout, b_yrw[tc]], writes=[byg[pb]], sem="dE")
        idma(xft[pb][:, :], None, cc_xf_out[:, :], IndirectOffsetOnAxis(xfr_sb[tc][:, :], 0),
             reads=[d_xfout, b_xfr[tc]], writes=[bxf2[pb]], sem="dE")
        t_r = vop(lambda Eg, pb=pb, tc=tc, rtE=rtE, lnE=lnE: Eg.scalar_tensor_tensor(
            rtE[:, :], ygt[pb][:, :], pscale[tc][:, 0:1], xft[pb][:, :],
            ALU.mult, ALU.add, accum_out=lnE[:, 0:1]),
            reads=[byg[pb], bxf2[pb], b_psc[tc]], writes=[b_rtE, b_lnE])
        t_v = sop(lambda Eg, rtE=rtE, sqE=sqE, lnE=lnE: Eg.activation(
            out=sqE[:, :], in_=rtE[:, :], func=AF.Square, accum_out=lnE[:, 2:3]),
            reads=[b_rtE], writes=[b_sqE, b_lnE])
        t_m = vop(lambda Eg, lnE=lnE: Eg.tensor_scalar(lnE[:, 1:2], lnE[:, 0:1], 1.0 / C,
                                                       None, ALU.mult),
                  reads=[b_lnE], writes=[b_lnE])
        t_e = vop(lambda Eg, lnE=lnE: Eg.tensor_scalar(lnE[:, 3:4], lnE[:, 2:3], 1.0 / C,
                                                       1e-5, ALU.mult, ALU.add),
                  reads=[b_lnE], writes=[b_lnE])
        t_s1 = vop(lambda Eg, lnE=lnE: Eg.tensor_scalar(
            lnE[:, 4:5], lnE[:, 1:2], lnE[:, 1:2], None, ALU.mult),
            reads=[b_lnE], writes=[b_lnE])
        t_s2 = vop(lambda Eg, lnE=lnE: Eg.tensor_sub(
            lnE[:, 5:6], lnE[:, 3:4], lnE[:, 4:5]),
            reads=[b_lnE], writes=[b_lnE])
        t_q = sop(lambda Eg, lnE=lnE: Eg.activation(out=lnE[:, 5:6], in_=lnE[:, 5:6],
                                                    func=AF.Ln),
                  reads=[b_lnE], writes=[b_lnE])
        t_i = sop(lambda Eg, lnE=lnE: Eg.activation(out=lnE[:, 5:6], in_=lnE[:, 5:6],
                                                    func=AF.Exp, scale=-0.5),
                  reads=[b_lnE], writes=[b_lnE])
        t_s3 = vop(lambda Eg, lnE=lnE: Eg.tensor_scalar(
            lnE[:, 6:7], lnE[:, 1:2], lnE[:, 5:6], None, ALU.mult),
            reads=[b_lnE], writes=[b_lnE])
        t_x = vop(lambda Eg, pb=pb, rtE=rtE, lnE=lnE: Eg.tensor_scalar(
            outE[pb][:, :], rtE[:, :], lnE[:, 5:6], lnE[:, 6:7], ALU.mult, ALU.subtract),
            reads=[b_rtE, b_lnE], writes=[bout[pb]])
        dma("sync", out_h[tc * 128:(tc + 1) * 128, :], outE[pb][:, :],
            reads=[bout[pb]], writes=[d_out] if tc == 0 else [],
            extra=() if tc == 0 else (d_out.w,), sem="dO")
        if tc == NTC - 1:
            d_out.w = ("dO", P.cnt["dO"])

    P.full_barrier(exclude=())
    return nc, P


_NC_CACHE = {}


def get_nc(debug_stage=DEBUG_STAGE):
    if debug_stage not in _NC_CACHE:
        nc, P = build_nc(debug_stage)
        sems = {}
        names = sorted(P.cnt.keys())
        with ExitStack() as sem_es:
            for n in names:
                sems[n] = sem_es.enter_context(nc.semaphore(n))
            with nc.Block() as block:
                @block.sync
                def _(sync):
                    P.replay("sync", sync, sems)
                @block.tensor
                def _(tensor):
                    P.replay("tensor", tensor, sems)
                @block.vector
                def _(vector):
                    P.replay("vector", vector, sems)
                @block.scalar
                def _(scalar):
                    P.replay("scalar", scalar, sems)
                @block.gpsimd
                def _(gpsimd):
                    P.replay("gpsimd", gpsimd, sems)
        _NC_CACHE[debug_stage] = nc
    return _NC_CACHE[debug_stage]


def make_in_maps(inputs):
    x = np.ascontiguousarray(np.asarray(inputs["L0_attn_x"], dtype=np.float32).reshape(-1, C))
    Wq = np.asarray(inputs["L0_attn_Wq"], dtype=np.float32)
    Wk = np.asarray(inputs["L0_attn_Wk"], dtype=np.float32)
    Wv = np.asarray(inputs["L0_attn_Wv"], dtype=np.float32)
    Wo = np.asarray(inputs["L0_attn_Wo"], dtype=np.float32)
    bq = np.asarray(inputs["L0_attn_bq"], dtype=np.float32)
    bk = np.asarray(inputs["L0_attn_bk"], dtype=np.float32)
    bv = np.asarray(inputs["L0_attn_bv"], dtype=np.float32)
    bo = np.asarray(inputs["L0_attn_bo"], dtype=np.float32)
    Wr = np.ascontiguousarray(np.asarray(inputs["L0_ffn_Wr"], dtype=np.float32))
    W1 = np.asarray(inputs["L0_ffn_W1"], dtype=np.float32)
    W2 = np.asarray(inputs["L0_ffn_W2"], dtype=np.float32)

    import ml_dtypes
    wqT = np.ascontiguousarray(Wq.T); wkT = np.ascontiguousarray(Wk.T)
    wvT = np.ascontiguousarray(Wv.T); woT = np.ascontiguousarray(Wo.T)
    wo = np.ascontiguousarray(Wo)
    bq_cols = np.ascontiguousarray(bq.reshape(NIC, 128).T)
    bo_cols = np.ascontiguousarray(bo.reshape(NIC, 128).T)
    bk_row = np.ascontiguousarray(bk.reshape(1, C))
    bv_row = np.ascontiguousarray(bv.reshape(1, C))
    bo_row = np.ascontiguousarray(bo.reshape(1, C))
    p = np.arange(128)
    ltri = (p[:, None] < p[None, :]).astype(np.float32)
    onesb_row = np.ones((1, 128), dtype=ml_dtypes.bfloat16)
    onesb_col = np.ones((128, 1), dtype=ml_dtypes.bfloat16)
    ones_col = np.ones((128, 1), dtype=np.float32)
    ones_1x128 = np.ones((1, 128), dtype=np.float32)
    iota8 = np.tile(np.arange(E, dtype=np.float32), (128, 1))
    trash = (16384.0 + p[:, None]).astype(np.float32)
    bdmask = ((p[:, None] < 64) == (p[None, :] < 64)).astype(np.float32)
    hmask2 = np.stack([(p < 64).astype(np.float32), (p >= 64).astype(np.float32)], axis=1)
    bmask = np.zeros((2, 128), dtype=ml_dtypes.bfloat16)
    bmask[0, :64] = 1; bmask[1, 64:] = 1
    iota132c = np.tile(np.arange(132, dtype=np.float32), (128, 1))
    iota128f = np.tile(np.arange(128, dtype=np.float32), (128, 1))

    in_maps = []
    for c in range(NCORE):
        sl = slice(c * TPC, (c + 1) * TPC)
        x_tm = np.ascontiguousarray(x[sl])
        x_fm = np.ascontiguousarray(x_tm.T)
        gid = (c * TPC + np.arange(TPC, dtype=np.int32)).reshape(TPC, 1)
        _l = np.arange(TPC, dtype=np.int32)
        xfrow = ((_l >> 9) * 4096 + c * 512 + (_l & 511)).astype(np.int32).reshape(TPC, 1)
        gidf = gid.astype(np.float32)
        tabidx = ((gid & 127) * 132 + (gid >> 7)).astype(np.int32)
        pair = c // 2
        kvidx = np.concatenate([
            (2 * pair) * 128 + np.arange(128, dtype=np.int32),
            (2 * pair + 1) * 128 + np.arange(128, dtype=np.int32)]).reshape(256, 1)
        ltme = (np.arange(NCORE) < c).astype(np.float32).reshape(NCORE, 1)
        in_maps.append({
            "x_tm": x_tm, "x_fm": x_fm,
            "wqT": wqT, "wkT": wkT, "wvT": wvT, "woT": woT, "wo": wo,
            "bq_cols": bq_cols, "bo_cols": bo_cols,
            "bk_row": bk_row, "bv_row": bv_row, "bo_row": bo_row,
            "wr": Wr,
            "w1e": np.ascontiguousarray(W1[c]),
            "w2e": np.ascontiguousarray(W2[c]),
            "gid": gid, "gidf": gidf, "xfrow": xfrow, "tabidx": tabidx, "kvidx": kvidx, "ltme": ltme,
            "ltri": ltri, "onesb_row": onesb_row, "onesb_col": onesb_col,
            "ones_col": ones_col, "ones_1x128": ones_1x128,
            "iota8": iota8, "trash": trash, "bdmask": bdmask,
            "hmask2": hmask2, "bmask": bmask,
            "iota132": iota132c, "iota128f": iota128f,
        })
    return in_maps


def kernel(**inputs):
    nc = get_nc(DEBUG_STAGE)
    in_maps = make_in_maps(inputs)
    res = run_bass_kernel_spmd(nc, in_maps, core_ids=list(range(NCORE)))
    out = np.concatenate([r["y"] for r in res.results], axis=0)
    return out.reshape(B, T, C)
